# version 12 (final)
# baseline (speedup 1.0000x reference)
import contextlib
import numpy as np
import ml_dtypes
import concourse.bass as bass
import concourse.mybir as mybir
from concourse.bass_utils import run_bass_kernel_spmd

F32 = mybir.dt.float32
BF16 = mybir.dt.bfloat16
I32 = mybir.dt.int32
AF = mybir.ActivationFunctionType
ALU = mybir.AluOpType
AX = mybir.AxisListType

NCORES = 8
D = 2048
TF = 4096
TL = 2048
DC = 16
DH = 8
TC = 512
NTC_G = TF // TC
NTC_L = TL // TC
IC0 = 5632 // 128
IC1 = 5504 // 128
GW = 768
NB_G = 6
PAIRS = [[0, 1], [2, 3], [4, 5], [6, 7]]
ALL8 = [[0, 1, 2, 3, 4, 5, 6, 7]]

bf16 = ml_dtypes.bfloat16
DEBUG_OUTS = set()


class Prog:
    ENGINES = ("tensor", "vector", "scalar", "gpsimd", "sync")

    def __init__(self, nc, stack):
        self.nc = nc
        self.sems = {}
        self.cnt = {}
        self.dsems = {}
        self.dcnt = {}
        self.thunks = {e: [] for e in self.ENGINES}
        self.last_ev = {e: None for e in self.ENGINES}
        for e in self.ENGINES:
            self.sems[e] = stack.enter_context(nc.semaphore(f"sem_{e}"))
            self.dsems[e] = stack.enter_context(nc.semaphore(f"dsem_{e}"))
            self.cnt[e] = 0
            self.dcnt[e] = 0
        self.ccsem = stack.enter_context(nc.semaphore("ccsem"))
        self.cccnt = 0

    def op(self, eng, fn, deps=(), inc=1, chain=True, dma=False, cc=False):
        if cc:
            my_sem = self.ccsem
        elif dma:
            my_sem = self.dsems[eng]
        else:
            my_sem = self.sems[eng]
        seen = {}
        auto = []
        if chain and eng != "tensor" and self.last_ev[eng] is not None:
            auto = [self.last_ev[eng]]
        for ev in list(deps) + auto:
            if ev is None:
                continue
            sem, val = ev
            key = id(sem)
            if key not in seen or seen[key][1] < val:
                seen[key] = (sem, val)
        dep_list = list(seen.values())
        if cc:
            self.cccnt += inc
            n_after = self.cccnt
        elif dma:
            self.dcnt[eng] += inc
            n_after = self.dcnt[eng]
        else:
            self.cnt[eng] += inc
            n_after = self.cnt[eng]

        def thunk(e, fn=fn, dep_list=dep_list, inc=inc, my_sem=my_sem):
            for sem, val in dep_list:
                e.wait_ge(sem, val)
            inst = fn(e)
            if inc:
                if isinstance(inst, (list, tuple)):
                    inst = inst[-1]
                inst.then_inc(my_sem, inc)

        self.thunks[eng].append(thunk)
        ev = (my_sem, n_after)
        self.last_ev[eng] = ev
        return ev

    def dma(self, eng, out, in_, deps=(), **kw):
        return self.op(eng, lambda e: e.dma_start(out=out, in_=in_, **kw),
                       deps=deps, inc=16, dma=True)

    def cc(self, kind, op_, groups, ins, outs, deps=()):
        return self.op("gpsimd", lambda e: e.collective_compute(
            kind, op_, replica_groups=groups, ins=ins, outs=outs),
            deps=deps, inc=1, cc=True)

    def ig(self, out_sb, view, idx_ap, elem_off, deps=()):
        return self.op("gpsimd", lambda e: e.indirect_dma_start(
            out=out_sb, out_offset=None, in_=view,
            in_offset=bass.IndirectOffsetOnAxis(ap=idx_ap, axis=0),
            element_offset=elem_off), deps=deps, inc=16, dma=True)

    def join(self):
        evs = []
        for e in self.ENGINES:
            if self.cnt[e]:
                evs.append((self.sems[e], self.cnt[e]))
            if self.dcnt[e]:
                evs.append((self.dsems[e], self.dcnt[e]))
        if self.cccnt:
            evs.append((self.ccsem, self.cccnt))
        return evs

    def final_barrier(self):
        for e in self.ENGINES:
            if not self.thunks[e]:
                continue
            me = self

            def thunk(eng, e=e):
                if me.cnt[e]:
                    eng.wait_ge(me.sems[e], me.cnt[e])
                if me.dcnt[e]:
                    eng.wait_ge(me.dsems[e], me.dcnt[e])
                if e == "gpsimd" and me.cccnt:
                    eng.wait_ge(me.ccsem, me.cccnt)

            self.thunks[e].append(thunk)

    def finalize(self, block):
        for e in self.ENGINES:
            if not self.thunks[e]:
                continue
            thunks = self.thunks[e]

            def body(engine, thunks=thunks):
                for t in thunks:
                    t(engine)

            getattr(block, e)(body)


def emit_fused(nc, P, st):
    dram = lambda name, shape, dt_, kind=None: nc.dram_tensor(
        name, shape, dt_, kind=(kind or "Internal"))
    x0 = dram("x0", [D, TL], F32, "ExternalInput")
    x1h = dram("x1h", [D, TL], F32, "ExternalInput")
    xl = dram("xl", [D, TL], F32, "ExternalInput")
    xg = dram("xg", [1024, TF], F32, "ExternalInput")
    xpw = dram("xpw", [D, 160], F32, "ExternalInput")
    dtw = dram("dtw", [128, 1024], F32, "ExternalInput")
    dtb = dram("dtb", [128, DH], F32, "ExternalInput")
    alog = dram("alog", [128, DH * 16], F32, "ExternalInput")
    dpar = dram("dpar", [128, DH], F32, "ExternalInput")
    gwt = dram("gwt", [128, IC0 * DC * 128], F32, "ExternalInput")
    uwt = dram("uwt", [128, IC0 * DC * 128], F32, "ExternalInput")
    dwt = dram("dwt", [128, DC * IC0 * 128], F32, "ExternalInput")
    gs_i = dram("gs", [128, IC0], F32, "ExternalInput")
    us_i = dram("us", [128, IC0], F32, "ExternalInput")
    ds_i = dram("ds", [128, DC], F32, "ExternalInput")
    wq_i = dram("wq", [128, 8 * DC * 128], F32, "ExternalInput")
    wk_i = dram("wk", [128, 2 * DC * 128], F32, "ExternalInput")
    wv_i = dram("wv", [128, 2 * DC * 128], F32, "ExternalInput")
    wo_i = dram("wo", [128, DC * DC * 128], F32, "ExternalInput")
    bo_i = dram("bo", [128, DC], F32, "ExternalInput")
    cosr_i = dram("cosr", [128, TF], F32, "ExternalInput")
    sins_i = dram("sins", [128, TF], F32, "ExternalInput")
    perm_i = dram("perm", [128, 128], F32, "ExternalInput")
    id2_i = dram("id2", [128, 64], F32, "ExternalInput")
    masks_i = dram("masks", [128, 4 * 512], F32, "ExternalInput")
    onesf_i = dram("onesf", [1, 128], F32, "ExternalInput")
    gi_i = dram("gi", [D, GW], I32, "ExternalInput")
    ui_i = dram("ui", [D, GW], I32, "ExternalInput")
    di_i = dram("di", [5504, 256], I32, "ExternalInput")
    lut_i = dram("lut", [1, 16], F32, "ExternalInput")
    idxp_i = dram("idxp", [128, 1], I32, "ExternalInput")
    selh_i = dram("selh", [1, 2], F32, "ExternalInput")
    out_o = dram("out", [D, TL], F32, "ExternalOutput")

    xn_b = dram("xn_b", [1024, TF], BF16)
    delta_d = dram("delta_d", [1024, TF], F32)
    bc_d = dram("bc_d", [32, TF], F32)
    rows_d = dram("rows_d", [1, 2 * TF], F32)
    ff_d = dram("ff_d", [D, TL], F32)
    s_d = dram("s_d", [1024, TF], F32)
    ag_s = dram("ag_s", [2048, TF], F32)
    xn1_d = dram("xn1_d", [D, TL], BF16)
    ag_xn1 = dram("ag_xn1", [2 * D, TL], BF16)
    x1_d = dram("x1_d", [D, TL], F32)
    ctx_d = dram("ctx_d", [1024, TF], BF16)
    ag_ctx = dram("ag_ctx", [2048, TF], BF16)
    cg_d = dram("cg_d", [128, NB_G * DC * 128], BF16)
    cu_d = dram("cu_d", [128, NB_G * DC * 128], BF16)
    cd_d = dram("cd_d", [128, 2 * IC1 * 128], BF16)
    ag_g = dram("ag_g", [8 * 128, NB_G * DC * 128], BF16)
    ag_u = dram("ag_u", [8 * 128, NB_G * DC * 128], BF16)
    ag_d = dram("ag_d", [8 * 128, 2 * IC1 * 128], BF16)
    gwt_b = dram("gwt_b", [128, IC0 * DC * 128], BF16)
    uwt_b = dram("uwt_b", [128, IC0 * DC * 128], BF16)
    dwt_b = dram("dwt_b", [128, DC * IC0 * 128], BF16)
    wq_b = dram("wq_b", [128, 8 * DC * 128], BF16)
    wk_b = dram("wk_b", [128, 2 * DC * 128], BF16)
    wv_b = dram("wv_b", [128, 2 * DC * 128], BF16)
    wo_b = dram("wo_b", [128, DC * DC * 128], BF16)

    x0_t = x0.ap().rearrange("(c p) t -> p c t", p=128)
    x1h_t = x1h.ap().rearrange("(c p) t -> p c t", p=128)
    xl_t = xl.ap().rearrange("(c p) t -> p c t", p=128)
    xg_t = xg.ap().rearrange("(c p) t -> p c t", p=128)
    xpw_t = xpw.ap().rearrange("(c p) e -> p c e", p=128)
    xn_b_t = xn_b.ap().rearrange("(c p) t -> p c t", p=128)
    delta_t = delta_d.ap().rearrange("(c p) t -> p c t", p=128)
    s_d_t = s_d.ap().rearrange("(c p) t -> p c t", p=128)
    ff_t = ff_d.ap().rearrange("(c p) t -> p c t", p=128)
    x1_t = x1_d.ap().rearrange("(c p) t -> p c t", p=128)
    xn1_t = xn1_d.ap().rearrange("(c p) t -> p c t", p=128)
    gwt_t = gwt_b.ap().rearrange("p (ic w) -> p ic w", ic=IC0)
    uwt_t = uwt_b.ap().rearrange("p (ic w) -> p ic w", ic=IC0)
    dwt_t = dwt_b.ap().rearrange("p (hc w) -> p hc w", hc=DC)
    wq_t = wq_b.ap().rearrange("p (oc w) -> p oc w", oc=8)
    wk_t = wk_b.ap().rearrange("p (oc w) -> p oc w", oc=2)
    wv_t = wv_b.ap().rearrange("p (oc w) -> p oc w", oc=2)
    wo_t = wo_b.ap().rearrange("p (oc w) -> p oc w", oc=DC)
    gi_t = gi_i.ap().rearrange("(c p) w -> p c w", p=128)
    ui_t = ui_i.ap().rearrange("(c p) w -> p c w", p=128)
    di_t = di_i.ap().rearrange("(c p) w -> p c w", p=128)
    cg_t = cg_d.ap().rearrange("p (lc c i) -> p lc c i", lc=NB_G, c=DC)
    cu_t = cu_d.ap().rearrange("p (lc c i) -> p lc c i", lc=NB_G, c=DC)
    cd_t = cd_d.ap().rearrange("p (oc ic o) -> p oc ic o", oc=2, ic=IC1)
    xn_glob = ag_xn1.ap().rearrange("(r c p) t -> p r c t", p=128, c=DC)
    agg_v = ag_g.ap().rearrange("(r p) (lc w) -> p r lc w", p=128, lc=NB_G)
    agu_v = ag_u.ap().rearrange("(r p) (lc w) -> p r lc w", p=128, lc=NB_G)
    agd_v = ag_d.ap().rearrange("(r p) (oc w) -> p r oc w", p=128, oc=2)
    ags_flat = ag_s.ap().rearrange("r (k s) -> (r k) s", s=2048)
    agc_flat = ag_ctx.ap().rearrange("r (k s) -> (r k) s", s=2048)
    rows_flat = rows_d.ap().rearrange("r (k s) -> (r k) s", s=2048)
    out_t = out_o.ap().rearrange("(c p) t -> p c t", p=128)

    sbp = lambda stk: (lambda name, shape, dt_: stk.enter_context(nc.sbuf_tensor(name, shape, dt_)))
    psp = lambda stk: (lambda name, shape, dt_=F32: stk.enter_context(nc.psum_tensor(name, shape, dt_)))

    sb = sbp(st)
    ones_f = sb("ones_f", [1, 128], F32)
    ones_b = sb("ones_b", [128, 1], BF16)
    cst = sb("cst", [128, 2], F32)
    idxp = sb("idxp_sb", [128, 1], I32)
    rstd_l = sb("rstd_l", [1, TL], F32)
    brow_l = sb("brow_l", [1, TL], F32)
    rowh = [sb(f"rowh{i}", [1, TC], F32) for i in range(2)]
    selh = sb("selh_sb", [1, 2], F32)

    ev_of = P.dma("sync", ones_f[:, :], onesf_i.ap()[:, :])
    ev_idx = P.dma("sync", idxp[:, :], idxp_i.ap()[:, :])
    ev_selh = P.dma("sync", selh[:, :], selh_i.ap()[:, :])
    ev_ob = P.op("vector", lambda e: e.memset(ones_b[:, :], 1.0))
    P.op("vector", lambda e: e.memset(cst[:, 0:1], 1e-5))
    ev_cst = P.op("vector", lambda e: e.memset(cst[:, 1:2], 1.0))

    wcast_evs = {}
    for nm, srcw, dstw, nch in (("g", gwt, gwt_b, 8), ("u", uwt, uwt_b, 8),
                                ("d", dwt, dwt_b, 8), ("q", wq_i, wq_b, 2),
                                ("k", wk_i, wk_b, 1), ("v", wv_i, wv_b, 1),
                                ("o", wo_i, wo_b, 4)):
        cols = srcw.ap().tensor.shape[1]
        cc_ = cols // nch
        for k in range(nch):
            wcast_evs[nm] = P.dma("gpsimd", dstw.ap()[:, k * cc_:(k + 1) * cc_],
                                  srcw.ap()[:, k * cc_:(k + 1) * cc_])

    gst = st.enter_context(contextlib.ExitStack())
    gsb = sbp(gst)
    GBW = 1024
    g_idx = gsb("g_idx", [128, GBW], I32)
    g_ibf = gsb("g_ibf", [128, GBW], BF16)
    g_tmp = gsb("g_tmp", [128, GBW], BF16)
    g_acc = gsb("g_acc", [128, GBW], BF16)
    g_lut = gsb("g_lut", [128, 16], F32)
    ev_lut = P.dma("gpsimd", g_lut[:, :], lut_i.ap()[:, :].partition_broadcast(128))

    ghost_jobs = [(gi_t, cg_t, DC, GW, NB_G, 1), (ui_t, cu_t, DC, GW, NB_G, 1),
                  (di_t, cd_t, IC1, 256, 2, 4)]
    ghost_groups = []
    for j, job in enumerate(ghost_jobs):
        nch, gsz = job[2], job[5]
        ch = 0
        while ch < nch:
            ghost_groups.append((j, ch, min(gsz, nch - ch)))
            ch += gsz
    ghost_state = dict(pos=0, idx_free=None, acc_free=None)
    ghost_store_evs = {0: [], 1: [], 2: []}

    def emit_ghost_group(j, ch, g):
        src_t, dst_t, nch, w_, ntile, gsz = ghost_jobs[j]
        W = g * w_
        ev_ld = P.dma("scalar", g_idx[:, 0:W].rearrange("p (c w) -> p c w", c=g),
                      src_t[:, ch:ch + g, :], deps=[ghost_state["idx_free"]])
        ev_cv = P.op("vector", lambda e, W=W: e.tensor_copy(
            g_ibf[:, 0:W], g_idx[:, 0:W]), deps=[ev_ld])
        ghost_state["idx_free"] = ev_cv
        P.op("vector", lambda e, W=W: e.tensor_scalar(
            g_acc[:, 0:W], g_ibf[:, 0:W], 0.0, g_lut[:, 0:1], ALU.is_equal, ALU.mult),
            deps=[ev_lut, ghost_state["acc_free"]])
        for k_ in range(1, 16):
            P.op("vector", lambda e, k_=k_, W=W: e.tensor_scalar(
                g_tmp[:, 0:W], g_ibf[:, 0:W], float(k_), g_lut[:, k_:k_ + 1],
                ALU.is_equal, ALU.mult))
            P.op("vector", lambda e, W=W: e.tensor_add(
                g_acc[:, 0:W], g_acc[:, 0:W], g_tmp[:, 0:W]))
        ev_done = P.last_ev["vector"]
        for ci in range(g):
            ev_st = P.dma("scalar", dst_t[:, :, ch + ci, :],
                          g_acc[:, ci * w_:(ci + 1) * w_].rearrange("p (l i) -> p l i", l=ntile),
                          deps=[ev_done] if ci == 0 else ())
            ghost_store_evs[j].append(ev_st)
        ghost_state["acc_free"] = ev_st

    def next_ghost(k):
        while k > 0 and ghost_state["pos"] < len(ghost_groups):
            j, ch, g = ghost_groups[ghost_state["pos"]]
            emit_ghost_group(j, ch, g)
            ghost_state["pos"] += 1
            k -= 1

    ast = st.enter_context(contextlib.ExitStack())
    asb, aps = sbp(ast), psp(ast)
    a_xt = asb("a_xt", [128, DC * TC], F32)
    a_xbf = asb("a_xbf", [128, DC * TC], BF16)
    a_sq = asb("a_sq", [128, DC * TC], BF16)
    a_xn = asb("a_xn", [128, DC * TC], BF16)
    a_xg = asb("a_xg", [128, DH * TC], F32)
    a_xns = asb("a_xns", [128, DH * TC], BF16)
    a_dt = asb("a_dt", [128, DH * TC], F32)
    a_rows = asb("a_rows", [1, 5 * TC], F32)
    a_dt16 = asb("a_dt16", [128, TC], BF16)
    a_bcf = asb("a_bcf", [32, TC], F32)
    a_zf = asb("a_zf", [128, TC], F32)
    a_az = asb("a_az", [128, TC], F32)
    a_el = asb("a_el", [128, TC], F32)
    a_rz = asb("a_rz", [128, TC], F32)
    a_xpw = asb("a_xpw", [128, DC * 160], BF16)
    a_dtw = asb("a_dtw", [128, 1024], BF16)
    a_dtb = asb("a_dtb", [128, DH], F32)
    p_mean = aps("p_mean", [1, TC])
    p_msq = aps("p_msq", [1, TC])
    p_a = aps("p_a", [128, TC])
    p_b = aps("p_b", [128, TC])
    p_dt = aps("p_dt", [128, TC])
    p_bc = aps("p_bc", [32, TC])
    p_z = aps("p_z", [128, TC])

    ev_w1 = P.dma("gpsimd", a_xpw[:, :].rearrange("p (c e) -> p c e", e=160), xpw_t)
    ev_w2 = P.dma("gpsimd", a_dtw[:, :], dtw.ap()[:, :])
    ev_w3 = P.dma("sync", a_dtb[:, :], dtb.ap()[:, :])

    ev_prev_stats = None
    ev_prev_bcast = None
    ev_apply_last = None
    ev_sq = ev_xc = None
    ev_xg_used = None
    ev_e2 = None
    ev_evdt = None
    ev_z_read = None
    ev_rz = None
    ev_dl = None
    ev_dstore = None
    ev_xns_store = None
    ev_bc_store = None
    rows_store_evs = []
    ev_brow_prev = ev_r3_prev = ev_rg1_prev = ev_rg2_prev = None
    delta_store_evs = []
    xns_store_evs = []
    bc_store_evs = []

    R = lambda i: a_rows[0:1, i * TC:(i + 1) * TC]
    for tc in range(NTC_G):
        tsl = slice(tc * TC, (tc + 1) * TC)
        src = x0_t if tc < NTC_G // 2 else x1h_t
        lsl = slice((tc % 4) * TC, (tc % 4) * TC + TC)
        ev_x = P.dma("sync", a_xt[:, :].rearrange("p (c t) -> p c t", t=TC),
                     src[:, :, lsl], deps=[ev_sq, ev_xc, ev_apply_last])
        ev_xc = P.op("vector", lambda e: e.tensor_copy(a_xbf[:, :], a_xt[:, :]), deps=[ev_x])
        ev_sq = P.op("scalar", lambda e: e.activation(a_sq[:, :], a_xt[:, :], AF.Square), deps=[ev_x])
        for c in range(DC):
            last = c == DC - 1
            P.op("tensor", lambda e, c=c, last=last: e.matmul(
                p_mean[:, :], ones_b[:, :], a_xbf[:, c * TC:(c + 1) * TC],
                start=(c == 0), stop=last),
                deps=[ev_xc, ev_ob, ev_prev_stats] if c == 0 else (), inc=1 if last else 0)
        ev_mm_mean = P.last_ev["tensor"]
        for c in range(DC):
            last = c == DC - 1
            P.op("tensor", lambda e, c=c, last=last: e.matmul(
                p_msq[:, :], ones_b[:, :], a_sq[:, c * TC:(c + 1) * TC],
                start=(c == 0), stop=last),
                deps=[ev_sq, ev_prev_stats] if c == 0 else (), inc=1 if last else 0)
        ev_mm_msq = P.last_ev["tensor"]
        ev_r0 = P.op("scalar", lambda e: e.activation(R(0), p_mean[0:1, :], AF.Copy, scale=1.0 / D),
                     deps=[ev_mm_mean, ev_brow_prev])
        ev_r1 = P.op("scalar", lambda e: e.activation(R(1), p_msq[0:1, :], AF.Copy, scale=1.0 / D),
                     deps=[ev_mm_msq, ev_r3_prev])
        ev_r2 = P.op("scalar", lambda e: e.activation(R(2), R(0), AF.Square), deps=[ev_r0])
        ev_r3 = P.op("vector", lambda e: e.tensor_sub(R(2), R(1), R(2)), deps=[ev_r1, ev_r2])
        ev_r3_prev = ev_r3
        ev_r4 = P.op("scalar", lambda e: e.activation(R(3), R(2), AF.Ln, bias=cst[0:1, 0:1]),
                     deps=[ev_r3, ev_cst, ev_rg1_prev])
        ev_rstd = P.op("scalar", lambda e: e.activation(R(3), R(3), AF.Exp, scale=-0.5), deps=[ev_r4])
        ev_brow = P.op("vector", lambda e: e.tensor_mul(R(4), R(0), R(3)), deps=[ev_rstd, ev_rg2_prev])
        ev_brow_prev = ev_brow
        ev_prev_stats = ev_r1
        ev_rg1 = P.dma("scalar", rows_d.ap()[0:1, tsl], R(3), deps=[ev_rstd])
        ev_rg2 = P.dma("scalar", rows_d.ap()[0:1, TF + tc * TC:TF + (tc + 1) * TC], R(4), deps=[ev_brow])
        ev_rg1_prev, ev_rg2_prev = ev_rg1, ev_rg2
        rows_store_evs.append(ev_rg2)
        bd = [ev_rstd, ev_brow, ev_of, ev_prev_bcast]
        ev_ba = P.op("tensor", lambda e: e.matmul(p_a[:, :], ones_f[:, :], R(3), start=True, stop=True), deps=bd)
        ev_bb = P.op("tensor", lambda e: e.matmul(p_b[:, :], ones_f[:, :], R(4), start=True, stop=True), deps=bd)
        dep0 = [ev_ba, ev_bb, ev_x]
        ev_apply_evs = []
        for c in range(DC):
            csl = slice(c * TC, (c + 1) * TC)
            e1 = P.op("vector", lambda e, csl=csl: e.tensor_mul(a_el[:, :], a_xt[:, csl], p_a[:, :]),
                      deps=dep0 if c == 0 else [ev_dl])
            e2 = P.op("vector", lambda e, csl=csl: e.tensor_sub(a_xn[:, csl], a_el[:, :], p_b[:, :]))
            ev_apply_evs.append(e2)
        ev_apply_full = P.last_ev["vector"]
        ev_xgl = P.dma("sync", a_xg[:, :].rearrange("p (c t) -> p c t", t=TC),
                       xg_t[:, :, tsl], deps=[ev_xg_used])
        for c in range(DH):
            csl = slice(c * TC, (c + 1) * TC)
            e1 = P.op("vector", lambda e, csl=csl: e.tensor_mul(a_el[:, :], a_xg[:, csl], p_a[:, :]),
                      deps=[ev_xgl, ev_xns_store])
            e2 = P.op("vector", lambda e, csl=csl: e.tensor_sub(a_xns[:, csl], a_el[:, :], p_b[:, :]))
        ev_xg_used = P.last_ev["vector"]
        ev_apply_last = ev_xg_used
        ev_prev_bcast = ev_xg_used
        ev_xns_store = P.dma("scalar", xn_b_t[:, :, tsl],
                             a_xns[:, :].rearrange("p (c t) -> p c t", t=TC),
                             deps=[ev_xg_used])
        xns_store_evs.append(ev_xns_store)
        for c in range(DC):
            last = c == DC - 1
            P.op("tensor", lambda e, c=c, last=last: e.matmul(
                p_dt[:, :], a_xpw[:, c * 160:c * 160 + 128], a_xn[:, c * TC:(c + 1) * TC],
                start=(c == 0), stop=last),
                deps=[ev_apply_full, ev_w1, ev_evdt, ev_e2] if c == 0 else (), inc=1 if last else 0)
        ev_mm_dt = P.last_ev["tensor"]
        for c in range(DC):
            last = c == DC - 1
            P.op("tensor", lambda e, c=c, last=last: e.matmul(
                p_bc[:, :], a_xpw[:, c * 160 + 128:(c + 1) * 160], a_xn[:, c * TC:(c + 1) * TC],
                start=(c == 0), stop=last), deps=(), inc=1 if last else 0)
        ev_mm_bc = P.last_ev["tensor"]
        ev_e2 = P.op("vector", lambda e: e.tensor_copy(a_dt16[:, :], p_dt[:, :]),
                     deps=[ev_mm_dt])
        ev_e3 = P.op("vector", lambda e: e.tensor_copy(a_bcf[:, :], p_bc[:, :]),
                     deps=[ev_mm_bc, ev_bc_store])
        ev_evdt = ev_e3
        ev_bc_store = P.dma("scalar", bc_d.ap()[:, tsl], a_bcf[:, :], deps=[ev_e3])
        bc_store_evs.append(ev_bc_store)
        ev_dl_all = []
        for dci in range(DH):
            ev_mm_z = P.op("tensor", lambda e, dci=dci: e.matmul(
                p_z[:, :], a_dtw[:, dci * 128:(dci + 1) * 128], a_dt16[:, :],
                start=True, stop=True), deps=[ev_e2, ev_w2, ev_z_read], inc=1)
            ev_zf = P.op("scalar", lambda e, dci=dci: e.activation(
                a_zf[:, :], p_z[:, :], AF.Identity, bias=a_dtb[:, dci:dci + 1]),
                deps=[ev_mm_z, ev_w3, ev_rz])
            ev_z_read = ev_zf
            ev_az = P.op("scalar", lambda e: e.activation(a_az[:, :], a_zf[:, :], AF.Abs), deps=[ev_zf])
            ev_el2 = P.op("scalar", lambda e: e.activation(a_el[:, :], a_az[:, :], AF.Exp, scale=-1.0),
                          deps=[ev_az, ev_dl, P.last_ev["vector"]])
            ev_l1 = P.op("scalar", lambda e: e.activation(a_el[:, :], a_el[:, :], AF.Ln, bias=cst[:, 1:2]), deps=[ev_el2])
            ev_rz = P.op("vector", lambda e: e.tensor_scalar(a_rz[:, :], a_zf[:, :], 0.0, None, ALU.max), deps=[ev_zf])
            ev_dl = P.op("vector", lambda e, dci=dci: e.tensor_add(
                a_dt[:, dci * TC:(dci + 1) * TC], a_rz[:, :], a_el[:, :]),
                deps=[ev_rz, ev_l1] + ([ev_dstore] if dci == 0 else []))
            ev_dl_all.append(ev_dl)
        ev_dstore = P.dma("scalar", delta_t[:, :, tsl],
                          a_dt[:, :].rearrange("p (c t) -> p c t", t=TC),
                          deps=[ev_dl_all[-1]])
        delta_store_evs.append(ev_dstore)
    ev_rows_store = rows_store_evs[-1]

    ev_endA = P.join()
    ast.close()

    evs_rg = []
    ev_bl_prev = None
    for which, dstrow in ((0, rstd_l), (1, brow_l)):
        base = which * TF
        for q in range(TL // TC):
            qsl = slice(q * TC, (q + 1) * TC)
            ev_rh0 = P.dma("sync", rowh[0][0:1, :], rows_d.ap()[0:1, base + q * TC:base + (q + 1) * TC],
                           deps=[ev_rows_store, ev_bl_prev])
            ev_rh1 = P.dma("sync", rowh[1][0:1, :], rows_d.ap()[0:1, base + TL + q * TC:base + TL + (q + 1) * TC])
            e1 = P.op("vector", lambda e, dstrow=dstrow, qsl=qsl: e.tensor_scalar(
                dstrow[0:1, qsl], rowh[0][0:1, :], selh[0:1, 0:1], None, ALU.mult),
                deps=[ev_rh0, ev_selh])
            e2 = P.op("vector", lambda e: e.tensor_scalar(
                rowh[1][0:1, :], rowh[1][0:1, :], selh[0:1, 1:2], None, ALU.mult),
                deps=[ev_rh1, ev_selh])
            e3 = P.op("vector", lambda e, dstrow=dstrow, qsl=qsl: e.tensor_add(
                dstrow[0:1, qsl], dstrow[0:1, qsl], rowh[1][0:1, :]), deps=[e1, e2])
            ev_bl_prev = e3
        evs_rg.append(ev_bl_prev)
    ev_rows_gather = evs_rg[0]
    ev_brow_copy = evs_rg[1]

    bst = st.enter_context(contextlib.ExitStack())
    bsb = sbp(bst)
    NG = 2
    DPG = DH // NG
    b_dlt = [bsb(f"b_dlt{i}", [128, DPG * TC], F32) for i in range(2)]
    b_xnt = [bsb(f"b_xnt{i}", [128, DPG * TC], BF16) for i in range(2)]
    b_ut = bsb("b_ut", [128, DPG * TC], F32)
    b_yt = bsb("b_yt", [128, DPG * TC], F32)
    bcb3 = [bsb(f"bcb3_{i}", [128, TC], F32) for i in range(3)]
    bcc3 = [bsb(f"bcc3_{i}", [128, TC], F32) for i in range(2)]
    b_at = [bsb(f"b_at{i}", [128, TC], F32) for i in range(2)]
    b_bbt = bsb("b_bbt", [128, TC], F32)
    b_ht = [bsb(f"b_ht{i}", [128, TC], F32) for i in range(2)]
    b_tmp = bsb("b_tmp", [128, TC], F32)
    b_A2 = bsb("b_A2", [128, DH * 16], F32)
    b_Dp = bsb("b_Dp", [128, DH], F32)
    b_carry = bsb("b_carry", [128, DH * 16], F32)

    ev_al = P.dma("sync", b_A2[:, :], alog.ap()[:, :], deps=ev_endA)
    ev_dp = P.dma("sync", b_Dp[:, :], dpar.ap()[:, :])
    ev_a1 = P.op("scalar", lambda e: e.activation(b_A2[:, :], b_A2[:, :], AF.Exp), deps=[ev_al] + ev_endA)
    ev_a2 = P.op("vector", lambda e: e.tensor_scalar(b_A2[:, :], b_A2[:, :], -1.0, None, ALU.mult),
                 deps=[ev_a1] + ev_endA)
    ev_cinit = P.op("vector", lambda e: e.memset(b_carry[:, :], 0.0))

    b_state = dict(ev_sstore=None, s_store_evs=[],
                   chunk_store=[None, None], bc_used=[None, None, None],
                   bcc_used=[None, None], si=0,
                   scan_p=[None, None], hc_p=[None, None])

    def emit_B_chunk(dcg, tc):
        k = dcg * NTC_G + tc
        cb = k % 2
        dlt, xnt = b_dlt[cb], b_xnt[cb]
        tsl = slice(tc * TC, (tc + 1) * TC)
        dsl = slice(dcg * DPG, (dcg + 1) * DPG)
        ev_dld = P.dma("sync", dlt[:, :].rearrange("p (c t) -> p c t", t=TC),
                       delta_t[:, dsl, tsl],
                       deps=[b_state["chunk_store"][cb], delta_store_evs[tc]])
        ev_xld = P.dma("sync", xnt[:, :].rearrange("p (c t) -> p c t", t=TC),
                       xn_b_t[:, dsl, tsl],
                       deps=[b_state["chunk_store"][cb], xns_store_evs[tc]])
        ev_yinit = []
        for j in range(DPG):
            csl = slice(j * TC, (j + 1) * TC)
            dci = dcg * DPG + j
            P.op("vector", lambda e, csl=csl, dlt=dlt, xnt=xnt: e.tensor_mul(
                dlt[:, csl] if False else b_ut[:, csl], dlt[:, csl], xnt[:, csl]),
                 deps=[ev_dld, ev_xld, b_state["ev_sstore"]])
            ev_yi = P.op("vector", lambda e, csl=csl, dci=dci, xnt=xnt: e.tensor_scalar(
                b_yt[:, csl], xnt[:, csl], b_Dp[:, dci:dci + 1], None, ALU.mult), deps=[ev_dp])
            ev_yinit.append(ev_yi)
        ev_y = None
        for n in range(16):
            nb = n % 3
            bcB, bcC = bcb3[nb], bcc3[n % 2]
            ev_bB = P.dma("sync", bcB[:, :], bc_d.ap()[n:n + 1, tsl].partition_broadcast(128),
                          deps=[b_state["bc_used"][nb], bc_store_evs[tc]])
            ev_bC = P.dma("sync", bcC[:, :], bc_d.ap()[16 + n:17 + n, tsl].partition_broadcast(128),
                          deps=[b_state["bcc_used"][n % 2]])
            ev_bb = None
            ev_hc = None
            for j in range(DPG):
                csl = slice(j * TC, (j + 1) * TC)
                dci = dcg * DPG + j
                idx = dci * 16 + n
                p = b_state["si"] % 2
                ev_a = P.op("scalar", lambda e, csl=csl, idx=idx, dlt=dlt, p=p: e.activation(
                    b_at[p][:, :], dlt[:, csl], AF.Exp, scale=b_A2[:, idx:idx + 1]),
                    deps=[ev_a2, ev_dld, b_state["scan_p"][p]])
                ev_bb = P.op("vector", lambda e, csl=csl, bcB=bcB: e.tensor_mul(
                    b_bbt[:, :], b_ut[:, csl], bcB[:, :]), deps=[ev_bB])
                ev_scan = P.op("vector", lambda e, idx=idx, p=p: e.tensor_tensor_scan(
                    b_ht[p][:, :], b_at[p][:, :], b_bbt[:, :], b_carry[:, idx:idx + 1],
                    ALU.mult, ALU.add),
                    deps=[ev_a, ev_cinit, b_state["hc_p"][p]])
                b_state["scan_p"][p] = ev_scan
                P.op("vector", lambda e, idx=idx, p=p: e.tensor_copy(
                    b_carry[:, idx:idx + 1], b_ht[p][:, TC - 1:TC]))
                ev_hc = P.op("gpsimd", lambda e, bcC=bcC, p=p: e.tensor_mul(
                    b_tmp[:, :], b_ht[p][:, :], bcC[:, :]), deps=[ev_scan, ev_bC])
                b_state["hc_p"][p] = ev_hc
                ev_y = P.op("gpsimd", lambda e, csl=csl: e.tensor_add(
                    b_yt[:, csl], b_yt[:, csl], b_tmp[:, :]),
                    deps=[ev_yinit[j]] if n == 0 else ())
                b_state["si"] += 1
            b_state["bc_used"][nb] = ev_bb
            b_state["bcc_used"][n % 2] = ev_hc
        for j in range(DPG):
            csl = slice(j * TC, (j + 1) * TC)
            P.op("vector", lambda e, csl=csl: e.tensor_scalar(b_yt[:, csl], b_yt[:, csl], 0.5, None, ALU.mult),
                 deps=[ev_y] if j == 0 else ())
        ev_st = P.dma("sync", s_d_t[:, dsl, tsl], b_yt[:, :].rearrange("p (c t) -> p c t", t=TC),
                      deps=[P.last_ev["vector"]])
        b_state["ev_sstore"] = ev_st
        b_state["chunk_store"][cb] = ev_st
        b_state["s_store_evs"].append(ev_st)

    cst_k = st.enter_context(contextlib.ExitStack())
    csb, cps = sbp(cst_k), psp(cst_k)
    c_xc = [csb(f"c_xc{i}", [128, TC], F32) for i in range(2)]
    c_xn = csb("c_xn", [128, DC * TC], BF16)
    c_hdn = csb("c_hdn", [128, IC0 * TC], BF16)
    c_gw = csb("c_gw", [128, DC * 128], BF16)
    c_uw = csb("c_uw", [128, DC * 128], BF16)
    c_dw = csb("c_dw", [128, IC0 * 128], BF16)
    c_gs = csb("c_gs", [128, IC0], F32)
    c_us = csb("c_us", [128, IC0], F32)
    c_ds = csb("c_ds", [128, DC], F32)
    c_gg = csb("c_gg", [128, TC], F32)
    c_uu = csb("c_uu", [128, TC], F32)
    c_th = csb("c_th", [128, TC], F32)
    c_t1 = csb("c_t1", [128, TC], F32)
    c_ff = csb("c_ff", [128, TC], F32)
    p_cg = cps("p_cg", [128, TC])
    p_cu = cps("p_cu", [128, TC])
    p_cd = cps("p_cd", [128, TC])
    p_ca = cps("p_ca", [128, TC])
    p_cb = cps("p_cb", [128, TC])


    ev_gs = P.dma("scalar", c_gs[:, :], gs_i.ap()[:, :], deps=ev_endA)
    ev_us = P.dma("scalar", c_us[:, :], us_i.ap()[:, :])
    ev_ds = P.dma("scalar", c_ds[:, :], ds_i.ap()[:, :])
    ev_us2 = P.op("vector", lambda e: e.tensor_scalar(c_us[:, :], c_us[:, :], 0.5, None, ALU.mult),
                  deps=[ev_us] + ev_endA)

    c_state = dict(ev_gw_used=None, ev_uw_used=None, ev_dw_used=None,
                   ev_evg=None, ev_evu=None, ev_th_r=None, ev_gg_r=None,
                   ev_ff_r=None, ev_hdn_used=None, ev_ffst=None,
                   ev_xn_used=None, ff_store_evs=[])

    def emit_Cfwd_tc(tc):
        tsl = slice(tc * TC, (tc + 1) * TC)
        ev_pa = P.op("tensor", lambda e, tsl=tsl: e.matmul(
            p_ca[:, :], ones_f[:, :], rstd_l[0:1, tsl], start=True, stop=True),
            deps=[ev_rows_gather, ev_of, c_state["ev_xn_used"]])
        ev_pb = P.op("tensor", lambda e, tsl=tsl: e.matmul(
            p_cb[:, :], ones_f[:, :], brow_l[0:1, tsl], start=True, stop=True),
            deps=[ev_brow_copy, c_state["ev_xn_used"]])
        ev_applies = []
        for c in range(DC):
            bi = c % 2
            ev_xcl = P.dma("scalar", c_xc[bi][:, :], xl_t[:, c, tsl],
                           deps=[ev_applies[-2] if c >= 2 else None])
            e1 = P.op("vector", lambda e, bi=bi: e.tensor_mul(c_ff[:, :], c_xc[bi][:, :], p_ca[:, :]),
                      deps=[ev_xcl, ev_pa, ev_pb, c_state["ev_ff_r"]])
            e2 = P.op("vector", lambda e, c=c: e.tensor_sub(c_xn[:, c * TC:(c + 1) * TC], c_ff[:, :], p_cb[:, :]))
            ev_applies.append(e2)
            c_state["ev_ff_r"] = e2
        ev_xn_ready = ev_applies[-1]
        for ic in range(IC0):
            ev_gwl = P.dma("scalar", c_gw[:, :], gwt_t[:, ic, :],
                           deps=[c_state["ev_gw_used"], wcast_evs["g"]])
            ev_uwl = P.dma("scalar", c_uw[:, :], uwt_t[:, ic, :],
                           deps=[c_state["ev_uw_used"], wcast_evs["u"]])
            for c in range(DC):
                last = c == DC - 1
                P.op("tensor", lambda e, c=c, last=last: e.matmul(
                    p_cg[:, :], c_gw[:, c * 128:(c + 1) * 128], c_xn[:, c * TC:(c + 1) * TC],
                    start=(c == 0), stop=last),
                    deps=[ev_gwl, ev_xn_ready, c_state["ev_evg"]] if c == 0 else (),
                    inc=1 if last else 0)
            ev_mm_g = P.last_ev["tensor"]
            c_state["ev_gw_used"] = ev_mm_g
            for c in range(DC):
                last = c == DC - 1
                P.op("tensor", lambda e, c=c, last=last: e.matmul(
                    p_cu[:, :], c_uw[:, c * 128:(c + 1) * 128], c_xn[:, c * TC:(c + 1) * TC],
                    start=(c == 0), stop=last),
                    deps=[ev_uwl, ev_xn_ready, c_state["ev_evu"]] if c == 0 else (),
                    inc=1 if last else 0)
            ev_mm_u = P.last_ev["tensor"]
            c_state["ev_uw_used"] = ev_mm_u
            ev_evg = P.op("vector", lambda e, ic=ic: e.tensor_scalar(
                c_gg[:, :], p_cg[:, :], c_gs[:, ic:ic + 1], None, ALU.mult),
                deps=[ev_mm_g, ev_gs, c_state["ev_gg_r"]])
            c_state["ev_evg"] = ev_evg
            ev_evu = P.op("vector", lambda e, ic=ic: e.tensor_scalar(
                c_uu[:, :], p_cu[:, :], c_us[:, ic:ic + 1], None, ALU.mult),
                deps=[ev_mm_u, ev_us2])
            c_state["ev_evu"] = ev_evu
            ev_th = P.op("scalar", lambda e: e.activation(c_th[:, :], c_gg[:, :], AF.Tanh, scale=0.5),
                         deps=[ev_evg, c_state["ev_th_r"]])
            c_state["ev_gg_r"] = ev_th
            ev_t1 = P.op("vector", lambda e: e.scalar_tensor_tensor(
                c_t1[:, :], c_th[:, :], 1.0, c_gg[:, :], ALU.add, ALU.mult), deps=[ev_th])
            c_state["ev_th_r"] = ev_t1
            ev_hdn = P.op("vector", lambda e, ic=ic: e.tensor_mul(
                c_hdn[:, ic * TC:(ic + 1) * TC], c_t1[:, :], c_uu[:, :]),
                deps=[ev_evu, c_state["ev_hdn_used"]])
        for hc in range(DC):
            ev_dwl = P.dma("scalar", c_dw[:, :], dwt_t[:, hc, :],
                           deps=[c_state["ev_dw_used"], wcast_evs["d"]])
            for ic in range(IC0):
                last = ic == IC0 - 1
                P.op("tensor", lambda e, ic=ic, last=last: e.matmul(
                    p_cd[:, :], c_dw[:, ic * 128:(ic + 1) * 128], c_hdn[:, ic * TC:(ic + 1) * TC],
                    start=(ic == 0), stop=last),
                    deps=[ev_dwl, P.last_ev["vector"]] if ic == 0 else (), inc=1 if last else 0)
            ev_mm_d = P.last_ev["tensor"]
            c_state["ev_dw_used"] = ev_mm_d
            if hc == DC - 1:
                c_state["ev_hdn_used"] = ev_mm_d
            ev_ff = P.op("vector", lambda e, hc=hc: e.tensor_scalar(
                c_ff[:, :], p_cd[:, :], c_ds[:, hc:hc + 1], None, ALU.mult),
                deps=[ev_mm_d, ev_ds, c_state["ev_ff_r"], c_state["ev_ffst"]])
            c_state["ev_ff_r"] = ev_ff
            ev_ffst = P.dma("scalar", ff_t[:, hc, tsl], c_ff[:, :], deps=[ev_ff])
            c_state["ev_ffst"] = ev_ffst
            c_state["ff_store_evs"].append(ev_ffst)
        c_state["ev_xn_used"] = P.last_ev["tensor"]

    for dcg in range(NG):
        for tc in range(NTC_G):
            emit_B_chunk(dcg, tc)
            k = dcg * NTC_G + tc
            if k % 4 == 1 and k // 4 < NTC_L:
                emit_Cfwd_tc(k // 4)

    ev_ag_s = P.cc("AllGather", ALU.bypass, PAIRS,
                   [s_d.ap().opt()], [ag_s.ap().opt()], deps=b_state["s_store_evs"])


    c_sg = csb("c_sg", [128, TC], F32)
    c_g2 = csb("c_g2", [128, TC], F32)
    c_xs = csb("c_xs", [128, TC], F32)
    c_x1 = csb("c_x1", [128, TC], F32)
    c_x1b = csb("c_x1b", [128, TC], BF16)
    c_sqb = csb("c_sqb", [128, TC], BF16)
    c_xn1 = csb("c_xn1", [128, TC], BF16)
    c_rows = csb("c_rows", [1, 4 * TC], F32)
    gpst = st.enter_context(contextlib.ExitStack())
    p_m1 = gpst.enter_context(nc.psum_tensor("p_m1", [1, TC], F32))
    p_m2 = gpst.enter_context(nc.psum_tensor("p_m2", [1, TC], F32))
    R1 = lambda i: c_rows[0:1, i * TC:(i + 1) * TC]

    ev_sg_free = None
    ev_brow1_prev = ev_q3_prev = ev_pa1_prev = ev_pb1_prev = None
    ev_x1_used = None
    ev_x1b_used = None
    ev_sqb_used = None
    ev_ln1_apply = None
    ev_th2 = None
    ev_xn1st = None
    x1_store_evs = []
    xn1_store_evs = []
    prev_stats1 = None
    for tc in range(NTC_L):
        tsl = slice(tc * TC, (tc + 1) * TC)
        ev_x1_stores = []
        for hc in range(DC):
            ev_ffl = P.dma("scalar", c_ff[:, :], ff_t[:, hc, tsl],
                           deps=[c_state["ev_ff_r"], c_state["ff_store_evs"][-1]])
            c_state["ev_ff_r"] = None
            ev_th2 = P.op("scalar", lambda e: e.activation(c_th[:, :], c_ff[:, :], AF.Tanh, scale=0.5),
                          deps=[ev_ffl, c_state["ev_th_r"]])
            ev_g2 = P.op("vector", lambda e: e.scalar_tensor_tensor(
                c_g2[:, :], c_th[:, :], 1.0, c_ff[:, :], ALU.add, ALU.mult), deps=[ev_th2])
            c_state["ev_th_r"] = ev_g2
            c_state["ev_ff_r"] = ev_g2
            ev_sgl = P.ig(c_sg[:, :], ags_flat, idxp[:, :],
                          hc * 128 * TF + tc * TC, deps=[ev_ag_s, ev_sg_free, ev_idx])
            ev_sg2 = P.op("vector", lambda e: e.tensor_mul(c_g2[:, :], c_g2[:, :], c_sg[:, :]),
                          deps=[ev_sgl])
            ev_sg_free = ev_sg2
            ev_xsl = P.dma("scalar", c_xs[:, :], xl_t[:, hc, tsl], deps=[P.last_ev["vector"]])
            ev_x1 = P.op("vector", lambda e: e.tensor_add(c_x1[:, :], c_xs[:, :], c_g2[:, :]),
                         deps=[ev_xsl, ev_x1_used])
            ev_x1st = P.dma("scalar", x1_t[:, hc, tsl], c_x1[:, :], deps=[ev_x1])
            ev_x1_stores.append(ev_x1st)
            ev_x1b = P.op("vector", lambda e: e.tensor_copy(c_x1b[:, :], c_x1[:, :]),
                          deps=[ev_x1, ev_x1b_used])
            ev_sqb = P.op("scalar", lambda e: e.activation(c_sqb[:, :], c_x1[:, :], AF.Square),
                          deps=[ev_x1, ev_sqb_used])
            ev_x1_used = ev_sqb
            P.op("tensor", lambda e, hc=hc: e.matmul(
                p_m1[:, :], ones_b[:, :], c_x1b[:, :], start=(hc == 0), stop=(hc == DC - 1)),
                deps=[ev_x1b, prev_stats1] if hc == 0 else [ev_x1b],
                inc=1 if hc == DC - 1 else 0)
            ev_mmm1 = P.last_ev["tensor"]
            P.op("tensor", lambda e, hc=hc: e.matmul(
                p_m2[:, :], ones_b[:, :], c_sqb[:, :], start=(hc == 0), stop=(hc == DC - 1)),
                deps=[ev_sqb, prev_stats1] if hc == 0 else [ev_sqb],
                inc=1 if hc == DC - 1 else 0)
            ev_mmm2 = P.last_ev["tensor"]
            ev_x1b_used = ev_mmm1
            ev_sqb_used = ev_mmm2
        ev_q0 = P.op("scalar", lambda e: e.activation(R1(0), p_m1[0:1, :], AF.Copy, scale=1.0 / D),
                     deps=[ev_mmm1, ev_brow1_prev])
        ev_q1 = P.op("scalar", lambda e: e.activation(R1(1), p_m2[0:1, :], AF.Copy, scale=1.0 / D),
                     deps=[ev_mmm2, ev_q3_prev, ev_pb1_prev])
        ev_q2 = P.op("scalar", lambda e: e.activation(R1(2), R1(0), AF.Square), deps=[ev_q0])
        ev_q3 = P.op("vector", lambda e: e.tensor_sub(R1(2), R1(1), R1(2)), deps=[ev_q1, ev_q2])
        ev_q3_prev = ev_q3
        ev_q4 = P.op("scalar", lambda e: e.activation(R1(3), R1(2), AF.Ln, bias=cst[0:1, 0:1]),
                     deps=[ev_q3, ev_pa1_prev])
        ev_rstd1 = P.op("scalar", lambda e: e.activation(R1(3), R1(3), AF.Exp, scale=-0.5), deps=[ev_q4])
        ev_brow1 = P.op("vector", lambda e: e.tensor_mul(R1(1), R1(0), R1(3)),
                        deps=[ev_rstd1, ev_pb1_prev, ev_q3])
        ev_brow1_prev = ev_brow1
        prev_stats1 = ev_q1
        ev_pa1 = P.op("tensor", lambda e: e.matmul(p_ca[:, :], ones_f[:, :], R1(3), start=True, stop=True),
                      deps=[ev_rstd1, ev_brow1, c_state["ev_xn_used"], ev_ln1_apply])
        ev_pb1 = P.op("tensor", lambda e: e.matmul(p_cb[:, :], ones_f[:, :], R1(1), start=True, stop=True),
                      deps=[ev_rstd1, ev_brow1, ev_ln1_apply])
        ev_pa1_prev, ev_pb1_prev = ev_pa1, ev_pb1
        for c in range(DC):
            bi = c % 2
            ev_x1l = P.dma("scalar", c_xc[bi][:, :], x1_t[:, c, tsl],
                           deps=[ev_x1_stores[c]] + ([P.last_ev["vector"]] if c >= 2 else []))
            e1 = P.op("vector", lambda e, bi=bi: e.tensor_mul(c_g2[:, :], c_xc[bi][:, :], p_ca[:, :]),
                      deps=[ev_x1l, ev_pa1, ev_pb1])
            e2 = P.op("vector", lambda e: e.tensor_sub(c_xn1[:, :], c_g2[:, :], p_cb[:, :]),
                      deps=[ev_xn1st])
            ev_ln1_apply = e2
            ev_xn1st = P.dma("scalar", xn1_t[:, c, tsl], c_xn1[:, :], deps=[e2])
            xn1_store_evs.append(ev_xn1st)
        x1_store_evs.extend(ev_x1_stores)

    ev_ag_xn1 = P.cc("AllGather", ALU.bypass, PAIRS,
                     [xn1_d.ap().opt()], [ag_xn1.ap().opt()], deps=xn1_store_evs)

    ev_endBC = P.join()
    gpst.close()
    cst_k.close()
    bst.close()

    dst = st.enter_context(contextlib.ExitStack())
    dsb, dps = sbp(dst), psp(dst)
    NOC_Q = 8
    d_xn = dsb("d_xn", [128, DC * TC], BF16)
    d_w = [dsb(f"d_w{i}", [128, DC * 128], BF16) for i in range(2)]
    q_sb = dsb("q_sb", [128, NOC_Q * TF], BF16)
    k_sb = dsb("k_sb", [128, 4 * TF], BF16)
    v_sb = dsb("v_sb", [128, 4 * 32 * 65], BF16)
    v_tmp = dsb("v_tmp", [128, TC], BF16)
    qf_bf = dsb("qf_bf", [128, TC], BF16)
    krope = dsb("krope", [128, TC], BF16)
    d_At = dsb("d_At", [128, TC], F32)
    cos_sb = dsb("cos_sb", [128, TC], F32)
    sin_sb = dsb("sin_sb", [128, TC], F32)
    perm_sb = dsb("perm_sb", [128, 128], BF16)
    id2_sb = dsb("id2_sb", [128, 64], BF16)
    masks_sb = dsb("masks_sb", [128, 4 * 512], BF16)
    q2t = dsb("q2t", [128, TC], BF16)
    qq = dsb("qq", [1, 32], F32)
    d_brow = dsb("d_brow", [1, 16], F32)
    biasT = dsb("biasT", [128, 16], F32)
    pT = [dsb(f"pT{i}", [128, TC], BF16) for i in range(4)]
    lrow = dsb("lrow", [1, TC], F32)
    ctxf = dsb("ctxf", [64, TC], F32)
    outbf = dsb("outbf", [64, TC], BF16)
    qpst = contextlib.ExitStack()
    p_mm = qpst.enter_context(nc.psum_tensor("p_mm", [128, TC], F32))
    p_sw = qpst.enter_context(nc.psum_tensor("p_sw", [128, TC], F32))
    p_T = qpst.enter_context(nc.psum_tensor("p_T", [128, 64], BF16))

    ev_perm = P.dma("gpsimd", perm_sb[:, :], perm_i.ap()[:, :], deps=ev_endBC)
    ev_id2 = P.dma("gpsimd", id2_sb[:, :], id2_i.ap()[:, :])
    ev_msk = P.dma("gpsimd", masks_sb[:, :], masks_i.ap()[:, :])
    ev_vones = P.op("vector", lambda e: e.memset(v_sb[:, :], 1.0), deps=ev_endBC)

    chunks = [("q", i) for i in range(NOC_Q)] + [("k", i) for i in range(2)] + [("v", i) for i in range(2)]
    ev_w_used = [None, None]
    ev_evac = None
    ev_rope_last = None
    ev_mmdone_prev = None
    ev_vt_done = None
    ev_cs_used = None
    ctx_store_evs = []
    for tc in range(NTC_G):
        tsl = slice(tc * TC, (tc + 1) * TC)
        r_, tcl = tc // 4, tc % 4
        lsl = slice(tcl * TC, (tcl + 1) * TC)
        ev_xnl = P.dma("sync", d_xn[:, :].rearrange("p (c t) -> p c t", t=TC),
                       xn_glob[:, r_, :, lsl],
                       deps=[ev_mmdone_prev, ev_ag_xn1] + (ev_endBC if tc == 0 else []))
        ev_cos = P.dma("sync", cos_sb[:, :], cosr_i.ap()[:, tsl], deps=[ev_cs_used])
        ev_sin = P.dma("sync", sin_sb[:, :], sins_i.ap()[:, tsl], deps=[ev_cs_used])
        for ci, (kind, oc) in enumerate(chunks):
            wbuf = ci % 2
            w_t = {"q": wq_t, "k": wk_t, "v": wv_t}[kind]
            ev_wld = P.dma("scalar", d_w[wbuf][:, :], w_t[:, oc, :],
                           deps=[ev_w_used[wbuf], wcast_evs[kind]])
            for c in range(DC):
                last = c == DC - 1
                P.op("tensor", lambda e, c=c, last=last, wbuf=wbuf: e.matmul(
                    p_mm[:, :], d_w[wbuf][:, c * 128:(c + 1) * 128], d_xn[:, c * TC:(c + 1) * TC],
                    start=(c == 0), stop=last),
                    deps=[ev_wld, ev_xnl, ev_evac] + (ev_endBC if tc == 0 and ci == 0 else []) if c == 0 else (),
                    inc=1 if last else 0)
            ev_mmq = P.last_ev["tensor"]
            ev_w_used[wbuf] = ev_mmq
            if kind == "v":
                ev_evac = P.op("vector", lambda e: e.tensor_copy(v_tmp[:, :], p_mm[:, :]),
                               deps=[ev_mmq, ev_vt_done])
                for half in range(2):
                    g = 2 * oc + half
                    hb = 64 * half
                    for j in range(4):
                        ev_tp = P.op("tensor", lambda e, hb=hb, j=j: e.transpose(
                            p_T[:, :], v_tmp[hb:hb + 64, j * 128:(j + 1) * 128], id2_sb[hb:hb + 64, :]),
                            deps=[ev_evac, ev_id2, ev_vt_done], inc=1)
                        ev_vt_done = P.op("vector", lambda e, g=g, j=j, tc=tc: e.tensor_copy(
                            v_sb[:, (g * 32 + tc * 4 + j) * 65:(g * 32 + tc * 4 + j) * 65 + 64],
                            p_T[:, :]), deps=[ev_tp, ev_vones])
            else:
                ev_evac = P.op("vector", lambda e: e.tensor_copy(qf_bf[:, :], p_mm[:, :]),
                               deps=[ev_mmq, ev_rope_last])
                ev_sw = P.op("tensor", lambda e: e.matmul(
                    p_sw[:, :], perm_sb[:, :], qf_bf[:, :], start=True, stop=True),
                    deps=[ev_evac, ev_perm, ev_rope_last], inc=1)
                ev_At2 = P.op("vector", lambda e: e.tensor_mul(d_At[:, :], qf_bf[:, :], cos_sb[:, :]),
                              deps=[ev_evac, ev_cos])
                if kind == "q":
                    dst_ap = q_sb[:, oc * TF + tc * TC: oc * TF + (tc + 1) * TC]
                    ev_r = P.op("vector", lambda e: e.tensor_mul(krope[:, :], p_sw[:, :], sin_sb[:, :]),
                                deps=[ev_sw, ev_sin])
                    ev_rope_last = P.op("vector", lambda e, dst_ap=dst_ap: e.tensor_add(
                        dst_ap, d_At[:, :], krope[:, :]))
                else:
                    ev_r = P.op("vector", lambda e: e.tensor_mul(krope[:, :], p_sw[:, :], sin_sb[:, :]),
                                deps=[ev_sw, ev_sin])
                    ev_rope_last = P.op("vector", lambda e: e.tensor_add(krope[:, :], d_At[:, :], krope[:, :]))
                    for half in range(2):
                        g = 2 * oc + half
                        hb = 64 * half
                        col = g * TF + tc * TC
                        P.dma("gpsimd", k_sb[0:64, col:col + TC], krope[hb:hb + 64, :], deps=[ev_rope_last])
                        ev_kdup = P.dma("gpsimd", k_sb[64:128, col:col + TC], krope[hb:hb + 64, :])
            ev_cs_used = ev_rope_last
        ev_mmdone_prev = P.last_ev["tensor"]
        next_ghost(1)
    ev_kdup_done = P.last_ev["gpsimd"]
    ev_qkv_done = [P.last_ev["vector"], P.last_ev["tensor"], P.last_ev["scalar"]]
    qpst.close()
    apst = st.enter_context(contextlib.ExitStack())
    p_s = [apst.enter_context(nc.psum_tensor(f"p_s{i}", [128, TC], F32)) for i in range(4)]
    p_ctx = apst.enter_context(nc.psum_tensor("p_ctx", [65, TC], F32))
    p_lb = apst.enter_context(nc.psum_tensor("p_lb", [64, TC], F32))
    p_n = apst.enter_context(nc.psum_tensor("p_n", [1, TC], F32))

    ev_sq_prev = None
    for idx in range(20):
        if idx < 16:
            qh = idx
            base, oc = 64 * (qh % 2), qh // 2
            src = lambda t_, base=base, oc=oc: q_sb[base:base + 64, oc * TF + t_ * TC: oc * TF + (t_ + 1) * TC]
            dep_data = ev_rope_last
        else:
            g = idx - 16
            base = 0
            src = lambda t_, g=g: k_sb[0:64, g * TF + t_ * TC: g * TF + (t_ + 1) * TC]
            dep_data = ev_kdup_done
        for t_ in range(NTC_G):
            ev_q2 = P.op("scalar", lambda e, t_=t_, base=base, src=src: e.activation(
                q2t[base:base + 64, :], src(t_), AF.Square),
                deps=[dep_data, ev_sq_prev] + (ev_qkv_done if idx == 0 and t_ == 0 else []))
            ev_nmm = P.op("tensor", lambda e, base=base: e.matmul(
                p_n[:, :], ones_b[base:base + 64, :], q2t[base:base + 64, :], start=True, stop=True),
                deps=[ev_q2, ev_ob, ev_sq_prev] + (ev_qkv_done if idx == 0 and t_ == 0 else []), inc=1)
            ev_red = P.op("vector", lambda e: e.reduce_max(
                qq[0:1, 31:32], p_n[0:1, :], axis=AX.X), deps=[ev_nmm])
            if t_ == 0:
                ev_mx = P.op("vector", lambda e, idx=idx: e.tensor_copy(qq[0:1, idx:idx + 1], qq[0:1, 31:32]))
            else:
                ev_mx = P.op("vector", lambda e, idx=idx: e.tensor_max(
                    qq[0:1, idx:idx + 1], qq[0:1, idx:idx + 1], qq[0:1, 31:32]))
            ev_sq_prev = ev_mx
    for qh in range(16):
        g = qh // 4
        P.op("vector", lambda e, qh=qh, g=g: e.tensor_mul(
            d_brow[0:1, qh:qh + 1], qq[0:1, qh:qh + 1], qq[0:1, 16 + g:17 + g]), deps=[ev_sq_prev])
    ev_ln2 = P.op("scalar", lambda e: e.activation(d_brow[0:1, :], d_brow[0:1, :], AF.Ln),
                  deps=[P.last_ev["vector"]])
    ev_sqr = P.op("scalar", lambda e: e.activation(d_brow[0:1, :], d_brow[0:1, :], AF.Exp, scale=0.5), deps=[ev_ln2])
    ev_bb2 = P.op("tensor", lambda e: e.matmul(p_lb[:, 0:16], ones_f[0:1, 0:64],
                                               d_brow[0:1, :], start=True, stop=True),
                  deps=[ev_sqr, ev_of], inc=1)
    ev_biasT = P.op("vector", lambda e: e.tensor_scalar(biasT[0:64, :], p_lb[:, 0:16], -0.125, None, ALU.mult),
                    deps=[ev_bb2])
    ev_biasT2 = P.dma("gpsimd", biasT[64:128, :], biasT[0:64, :], deps=[ev_biasT])

    ev_exp_done = [None, None, None, None]
    ev_pv_done = [None, None, None, None]
    ev_ctx_free = None
    ev_lb_free = ev_biasT2
    ev_ctxstore = None
    LKA = 3
    for qh in range(16):
        next_ghost(2)
        g = qh // 4
        base, oc = 64 * (qh % 2), qh // 2
        for qc in range(NTC_G):
            n_kt = 4 * qc + 4
            ev_s_map = {}
            for step in range(n_kt + LKA):
                kt = step
                if kt < n_kt:
                    buf = kt % 4
                    ev_s_map[kt] = P.op("tensor", lambda e, buf=buf, kt=kt, g=g, base=base, oc=oc, qc=qc: e.matmul(
                        p_s[buf][:, :],
                        k_sb[base:base + 64, g * TF + kt * 128: g * TF + (kt + 1) * 128],
                        q_sb[base:base + 64, oc * TF + qc * TC: oc * TF + (qc + 1) * TC],
                        start=True, stop=True),
                        deps=[ev_exp_done[buf], ev_rope_last, ev_kdup_done], inc=1)
                kd = step - LKA
                if kd < 0 or kd >= n_kt:
                    continue
                buf = kd % 4
                ev_e = P.op("scalar", lambda e, buf=buf, qh=qh: e.activation(
                    pT[buf][:, :], p_s[buf][:, :], AF.Exp, scale=0.125, bias=biasT[:, qh:qh + 1]),
                    deps=[ev_s_map.pop(kd), ev_pv_done[buf], ev_biasT2])
                ev_exp_done[buf] = ev_e
                ev_pin = ev_e
                if kd >= 4 * qc:
                    j = kd - 4 * qc
                    ev_pin = P.op("vector", lambda e, buf=buf, j=j: e.tensor_mul(
                        pT[buf][:, :], pT[buf][:, :], masks_sb[:, j * 512:(j + 1) * 512]),
                        deps=[ev_e, ev_msk])
                ev_pv = P.op("tensor", lambda e, buf=buf, kt=kd, g=g, n_kt=n_kt: e.matmul(
                    p_ctx[:, :], v_sb[:, (g * 32 + kt) * 65:(g * 32 + kt) * 65 + 65], pT[buf][:, :],
                    start=(kt == 0), stop=(kt == n_kt - 1)),
                    deps=[ev_pin, ev_vt_done] + ([ev_ctx_free] if kd == 0 else []), inc=1)
                ev_pv_done[buf] = ev_pv
            ev_li = P.op("vector", lambda e: e.reciprocal(lrow[0:1, :], p_ctx[64:65, :]), deps=[ev_pv])
            ev_lb = P.op("tensor", lambda e: e.matmul(
                p_lb[:, :], ones_f[0:1, 0:64], lrow[0:1, :], start=True, stop=True),
                deps=[ev_li, ev_lb_free], inc=1)
            ev_cf = P.op("vector", lambda e: e.tensor_copy(ctxf[:, :], p_ctx[0:64, :]), deps=[ev_pv])
            ev_ctx_free = ev_cf
            ev_ob2 = P.op("vector", lambda e: e.tensor_mul(outbf[:, :], ctxf[:, :], p_lb[:, :]),
                          deps=[ev_lb, ev_cf, ev_ctxstore])
            ev_lb_free = ev_ob2
            ev_ctxstore = P.dma("sync", ctx_d.ap()[qh * 64:(qh + 1) * 64, qc * TC:(qc + 1) * TC],
                                outbf[:, :], deps=[ev_ob2])
            ctx_store_evs.append(ev_ctxstore)

    next_ghost(len(ghost_groups))
    ev_ag_g = P.cc("AllGather", ALU.bypass, ALL8,
                   [cg_d.ap().opt()], [ag_g.ap().opt()], deps=ghost_store_evs[0])
    ev_ag_u = P.cc("AllGather", ALU.bypass, ALL8,
                   [cu_d.ap().opt()], [ag_u.ap().opt()], deps=ghost_store_evs[1])
    ev_ag_d = P.cc("AllGather", ALU.bypass, ALL8,
                   [cd_d.ap().opt()], [ag_d.ap().opt()], deps=ghost_store_evs[2])
    ev_ag_ctx = P.cc("AllGather", ALU.bypass, PAIRS,
                     [ctx_d.ap().opt()], [ag_ctx.ap().opt()], deps=ctx_store_evs)

    ev_endD = P.join()
    apst.close()
    dst.close()

    est = st.enter_context(contextlib.ExitStack())
    esb, eps = sbp(est), psp(est)
    e_xn = esb("e_xn", [128, DC * TC], BF16)
    e_ctx = esb("e_ctx", [128, DC * TC], BF16)
    e_s1 = esb("e_s1", [128, DC * TC], F32)
    e_hdn = esb("e_hdn", [128, IC1 * TC], BF16)
    e_wo = [esb(f"e_wo{i}", [128, DC * 128], BF16) for i in range(2)]
    e_wgu = [esb(f"e_wgu{i}", [128, DC * 128], BF16) for i in range(4)]
    e_wd = [esb(f"e_wd{i}", [128, IC1 * 128], BF16) for i in range(2)]
    e_bo = esb("e_bo", [128, DC], F32)
    e_xs = esb("e_xs", [128, TC], F32)
    e_gg = esb("e_gg", [128, TC], F32)
    e_uu = esb("e_uu", [128, TC], F32)
    e_th = esb("e_th", [128, TC], F32)
    e_t1 = esb("e_t1", [128, TC], F32)
    e_ff = esb("e_ff", [128, TC], F32)
    e_g2 = esb("e_g2", [128, TC], F32)
    e_out = esb("e_out", [128, TC], F32)
    p_eo = eps("p_eo", [128, TC])
    p_eg = eps("p_eg", [128, TC])
    p_eu = eps("p_eu", [128, TC])
    p_ed = eps("p_ed", [128, TC])

    ev_bo = P.dma("gpsimd", e_bo[:, :], bo_i.ap()[:, :], deps=ev_endD)

    ew_used = {}
    ev_s1_done = None
    ev_s1_war = None
    ev_ehdn_used = None
    ev_outstore = None
    ev_eth_r = None
    ev_egg_r = None
    ev_eff_r = None
    ev_evo = ev_evg2 = ev_evu2 = None
    ev_ctxg_prev = None
    for tc in range(NTC_L):
        tsl = slice(tc * TC, (tc + 1) * TC)
        ev_xnl2 = P.dma("sync", e_xn[:, :].rearrange("p (c t) -> p c t", t=TC),
                        xn1_t[:, :, tsl],
                        deps=[P.last_ev["tensor"]] + (ev_endD if tc == 0 else []) + [xn1_store_evs[-1]])
        ctx_g_evs = []
        for c in range(DC):
            evg = P.ig(e_ctx[:, c * TC:(c + 1) * TC], agc_flat, idxp[:, :],
                       c * 128 * TF + tc * TC,
                       deps=[ev_ag_ctx, ev_ctxg_prev, ev_idx] + (ev_endD if tc == 0 and c == 0 else []))
            ctx_g_evs.append(evg)
        ev_ctxg = ctx_g_evs[-1]
        for hc in range(DC):
            wb = hc % 2
            ev_wol = P.dma("scalar", e_wo[wb][:, :], wo_t[:, hc, :],
                           deps=[ew_used.get(("o", wb)), wcast_evs["o"]])
            for c in range(DC):
                last = c == DC - 1
                P.op("tensor", lambda e, c=c, last=last, wb=wb: e.matmul(
                    p_eo[:, :], e_wo[wb][:, c * 128:(c + 1) * 128], e_ctx[:, c * TC:(c + 1) * TC],
                    start=(c == 0), stop=last),
                    deps=[ev_wol, ev_ctxg, ev_xnl2, ev_evo] if c == 0 else (), inc=1 if last else 0)
            ev_mmo = P.last_ev["tensor"]
            ew_used[("o", wb)] = ev_mmo
            ev_evo = P.op("scalar", lambda e, hc=hc: e.activation(
                e_s1[:, hc * TC:(hc + 1) * TC], p_eo[:, :], AF.Identity, bias=e_bo[:, hc:hc + 1]),
                deps=[ev_mmo, ev_bo, ev_evo, ev_s1_war])
        ev_s1_done = ev_evo
        ev_ctxg_prev = P.last_ev["tensor"]
        for ic in range(IC1):
            r_, lc = ic // NB_G, ic % NB_G
            wbg, wbu = ic % 2, 2 + ic % 2
            ev_wgl = P.dma("scalar", e_wgu[wbg][:, :], agg_v[:, r_, lc, :],
                           deps=[ew_used.get(("g", wbg)), ev_ag_g])
            ev_wul = P.dma("scalar", e_wgu[wbu][:, :], agu_v[:, r_, lc, :],
                           deps=[ew_used.get(("u", wbu)), ev_ag_u])
            for c in range(DC):
                last = c == DC - 1
                P.op("tensor", lambda e, c=c, last=last, wbg=wbg: e.matmul(
                    p_eg[:, :], e_wgu[wbg][:, c * 128:(c + 1) * 128], e_xn[:, c * TC:(c + 1) * TC],
                    start=(c == 0), stop=last),
                    deps=[ev_wgl, ev_xnl2, ev_evg2] if c == 0 else (), inc=1 if last else 0)
            ev_mmg = P.last_ev["tensor"]
            ew_used[("g", wbg)] = ev_mmg
            for c in range(DC):
                last = c == DC - 1
                P.op("tensor", lambda e, c=c, last=last, wbu=wbu: e.matmul(
                    p_eu[:, :], e_wgu[wbu][:, c * 128:(c + 1) * 128], e_xn[:, c * TC:(c + 1) * TC],
                    start=(c == 0), stop=last),
                    deps=[ev_wul, ev_xnl2, ev_evu2] if c == 0 else (), inc=1 if last else 0)
            ev_mmu = P.last_ev["tensor"]
            ew_used[("u", wbu)] = ev_mmu
            ev_evg2 = P.op("vector", lambda e: e.tensor_copy(e_gg[:, :], p_eg[:, :]),
                           deps=[ev_mmg, ev_egg_r])
            ev_evu2 = P.op("vector", lambda e: e.tensor_copy(e_uu[:, :], p_eu[:, :]), deps=[ev_mmu])
            ev_eth = P.op("scalar", lambda e: e.activation(e_th[:, :], e_gg[:, :], AF.Tanh, scale=0.5),
                          deps=[ev_evg2, ev_eth_r])
            ev_egg_r = ev_eth
            ev_et1 = P.op("vector", lambda e: e.scalar_tensor_tensor(
                e_t1[:, :], e_th[:, :], 1.0, e_gg[:, :], ALU.add, ALU.mult), deps=[ev_eth])
            ev_eth_r = ev_et1
            P.op("vector", lambda e, ic=ic: e.tensor_mul(
                e_hdn[:, ic * TC:(ic + 1) * TC], e_t1[:, :], e_uu[:, :]), deps=[ev_ehdn_used])
        for hc in range(DC):
            r_, oc2 = hc // 2, hc % 2
            wb = hc % 2
            ev_wdl = P.dma("scalar", e_wd[wb][:, :], agd_v[:, r_, oc2, :],
                           deps=[ew_used.get(("d", wb)), ev_ag_d])
            ev_xsl2 = P.dma("sync", e_xs[:, :], x1_t[:, hc, tsl],
                            deps=[P.last_ev["vector"], x1_store_evs[tc * DC + hc]])
            for ic in range(IC1):
                last = ic == IC1 - 1
                P.op("tensor", lambda e, ic=ic, last=last, wb=wb: e.matmul(
                    p_ed[:, :], e_wd[wb][:, ic * 128:(ic + 1) * 128], e_hdn[:, ic * TC:(ic + 1) * TC],
                    start=(ic == 0), stop=last),
                    deps=[ev_wdl, P.last_ev["vector"]] if ic == 0 else (), inc=1 if last else 0)
            ev_mmd = P.last_ev["tensor"]
            ew_used[("d", wb)] = ev_mmd
            if hc == DC - 1:
                ev_ehdn_used = ev_mmd
            ev_eff = P.op("vector", lambda e: e.tensor_scalar(e_ff[:, :], p_ed[:, :], 0.5, None, ALU.mult),
                          deps=[ev_mmd, ev_eff_r])
            ev_eth2 = P.op("scalar", lambda e: e.activation(e_th[:, :], e_ff[:, :], AF.Tanh, scale=0.5),
                           deps=[ev_eff, ev_eth_r])
            ev_eff_r = ev_eth2
            ev_eg2 = P.op("vector", lambda e: e.scalar_tensor_tensor(
                e_g2[:, :], e_th[:, :], 1.0, e_ff[:, :], ALU.add, ALU.mult), deps=[ev_eth2])
            ev_eth_r = ev_eg2
            ev_esg = P.op("vector", lambda e, hc=hc: e.tensor_mul(
                e_g2[:, :], e_g2[:, :], e_s1[:, hc * TC:(hc + 1) * TC]), deps=[ev_s1_done])
            ev_eo = P.op("vector", lambda e: e.scalar_tensor_tensor(
                e_out[:, :], e_g2[:, :], 0.5, e_xs[:, :], ALU.mult, ALU.add),
                deps=[ev_xsl2, ev_outstore])
            ev_outstore = P.dma("sync", out_t[:, hc, tsl], e_out[:, :], deps=[ev_eo])
            ev_s1_war = ev_esg

    if DEBUG_OUTS:
        allt = {"xn_b": xn_b, "delta_d": delta_d, "bc_d": bc_d, "rows_d": rows_d,
                "ff_d": ff_d, "s_d": s_d, "ag_s": ag_s, "xn1_d": xn1_d,
                "ag_xn1": ag_xn1, "x1_d": x1_d, "ctx_d": ctx_d, "ag_ctx": ag_ctx,
                "cg_d": cg_d, "cu_d": cu_d, "cd_d": cd_d,
                "ag_g": ag_g, "ag_u": ag_u, "ag_d": ag_d,
                "gwt_b": gwt_b, "uwt_b": uwt_b, "dwt_b": dwt_b}
        endall = P.join()
        for name in sorted(DEBUG_OUTS):
            h = allt[name]
            shp = list(h.ap().tensor.shape)
            dbg = nc.dram_tensor("dbg_" + name, shp, h.ap().tensor.dtype,
                                 kind="ExternalOutput")
            P.dma("gpsimd", dbg.ap()[:, :], h.ap()[:, :], deps=endall)


def build_fused():
    nc = bass.Bass("TRN2", target_bir_lowering=False)
    with contextlib.ExitStack() as st:
        P = Prog(nc, st)
        emit_fused(nc, P, st)
        P.final_barrier()
        with nc.Block() as block:
            P.finalize(block)
    return nc


def _rope_consts():
    hd = 64
    inv_freq = 1.0 / (10000.0 ** (np.arange(0, hd, 2, dtype=np.float64) / hd))
    t = np.arange(TF, dtype=np.float64)
    fr = np.outer(t, inv_freq)
    cosT = np.cos(fr).T.astype(np.float32)
    sinT = np.sin(fr).T.astype(np.float32)
    cosr = np.tile(cosT, (4, 1))
    sins = np.concatenate([-sinT, sinT] * 2, axis=0)
    perm = np.zeros((128, 128), np.float32)
    for m_ in range(128):
        b_ = 64 * (m_ // 64)
        src = b_ + (m_ - b_ + 32) % 64
        perm[src, m_] = 1.0
    id2 = np.concatenate([np.eye(64, dtype=np.float32)] * 2, axis=0)
    masks = np.zeros((128, 4 * 512), np.float32)
    for j in range(4):
        for p_ in range(128):
            ts0 = j * 128 + p_
            if ts0 < 512:
                masks[p_, j * 512 + ts0:(j + 1) * 512] = 1.0
    return cosr, sins, perm, id2, masks


def kernel(**inputs):
    nc = build_fused()
    x_full = np.asarray(inputs["L0_attn_x"])
    cosr, sins, perm, id2, masks = _rope_consts()
    onesf = np.ones((1, 128), np.float32)
    def tile4(WT, noc):
        K, M = WT.shape
        assert M == noc * 128
        return np.ascontiguousarray(
            WT.reshape(K // 128, 128, noc, 128).transpose(1, 2, 0, 3).reshape(128, -1))

    xpw_T = np.ascontiguousarray(np.asarray(inputs["L0_attn_x_proj_w"]).T)
    dtw_full = np.asarray(inputs["L0_attn_dt_proj_w"])
    dtb_full = np.asarray(inputs["L0_attn_dt_proj_b"])
    alog_full = np.asarray(inputs["L0_attn_A_log"])
    dp_full = np.asarray(inputs["L0_attn_D_param"])
    gwt = tile4(np.asarray(inputs["L0_ffn_gate_w"]).T, IC0)
    uwt = tile4(np.asarray(inputs["L0_ffn_up_w"]).T, IC0)
    dwt = tile4(np.asarray(inputs["L0_ffn_down_w"]).T, DC)
    gs = np.ascontiguousarray(np.asarray(inputs["L0_ffn_gate_s"]).reshape(IC0, 128).T)
    us = np.ascontiguousarray(np.asarray(inputs["L0_ffn_up_s"]).reshape(IC0, 128).T)
    ds = np.ascontiguousarray(np.asarray(inputs["L0_ffn_down_s"]).reshape(DC, 128).T)
    Wq, Wk, Wv = (np.asarray(inputs["L1_attn_Wq"]), np.asarray(inputs["L1_attn_Wk"]),
                  np.asarray(inputs["L1_attn_Wv"]))
    Wo = np.asarray(inputs["L1_attn_Wo"])
    wo4 = None
    bo = np.ascontiguousarray(np.asarray(inputs["L1_attn_b_out"]).reshape(DC, 128).T)
    giT = np.asarray(inputs["L1_ffn_gate_idx"]).T
    uiT = np.asarray(inputs["L1_ffn_up_idx"]).T
    diT = np.asarray(inputs["L1_ffn_down_idx"]).T
    lut = np.ascontiguousarray(np.asarray(inputs["L1_ffn_lut"]).reshape(1, 16).astype(np.float32))

    def padcols(a, w):
        out = np.zeros((a.shape[0], w), a.dtype)
        take = min(w, a.shape[1])
        out[:, :take] = a[:, :take]
        return out

    wo4 = tile4(Wo, DC)
    maps = []
    for c in range(NCORES):
        b, h = c // 2, c % 2
        xT = np.ascontiguousarray(x_full[b].T)
        m = {
            "x0": np.ascontiguousarray(xT[:, 0:TL]),
            "x1h": np.ascontiguousarray(xT[:, TL:]),
            "xl": np.ascontiguousarray(xT[:, h * TL:(h + 1) * TL]),
            "xg": np.ascontiguousarray(xT[1024 * h:1024 * h + 1024, :]),
            "xpw": xpw_T,
            "dtw": np.ascontiguousarray(dtw_full[1024 * h:1024 * h + 1024, :].T),
            "dtb": np.ascontiguousarray(dtb_full[1024 * h:1024 * h + 1024].reshape(DH, 128).T),
            "alog": np.ascontiguousarray(
                alog_full[1024 * h:1024 * h + 1024].reshape(DH, 128, 16).transpose(1, 0, 2).reshape(128, DH * 16)),
            "dpar": np.ascontiguousarray(dp_full[1024 * h:1024 * h + 1024].reshape(DH, 128).T),
            "gwt": gwt, "uwt": uwt, "dwt": dwt, "gs": gs, "us": us, "ds": ds,
            "wq": tile4(Wq[:, 1024 * h:1024 * h + 1024], 8),
            "wk": tile4(Wk[:, 256 * h:256 * h + 256], 2),
            "wv": tile4(Wv[:, 256 * h:256 * h + 256], 2),
            "wo": wo4, "bo": bo,
            "cosr": cosr, "sins": sins, "perm": perm, "id2": id2, "masks": masks,
            "onesf": onesf,
            "gi": np.ascontiguousarray(padcols(giT[:, GW * c:GW * (c + 1)], GW)),
            "ui": np.ascontiguousarray(padcols(uiT[:, GW * c:GW * (c + 1)], GW)),
            "di": np.ascontiguousarray(diT[:, 256 * c:256 * (c + 1)]),
            "lut": lut,
            "idxp": (2 * np.arange(128, dtype=np.int32) + h).reshape(128, 1),
            "selh": np.array([[1.0 - h, float(h)]], np.float32),
        }
        maps.append(m)
    res = run_bass_kernel_spmd(nc, maps, core_ids=list(range(NCORES)))
    out = np.empty((4, TF, D), np.float32)
    for c in range(NCORES):
        b, h = c // 2, c % 2
        out[b, h * TL:(h + 1) * TL, :] = np.asarray(res.results[c]["out"]).T
    return out


if __name__ == "__main__":
    nc = build_fused()
    n = sum(1 for _ in nc.all_instructions())
    print("built ok, instructions:", n)


# version 13
# speedup vs baseline: 1.1337x; 1.1337x over previous
import contextlib
import numpy as np
import ml_dtypes
import concourse.bass as bass
import concourse.mybir as mybir
from concourse.bass_utils import run_bass_kernel_spmd

F32 = mybir.dt.float32
BF16 = mybir.dt.bfloat16
I32 = mybir.dt.int32
AF = mybir.ActivationFunctionType
ALU = mybir.AluOpType
AX = mybir.AxisListType

NCORES = 8
D = 2048
TF = 4096
TL = 2048
DC = 16
DH = 8
TC = 512
NTC_G = TF // TC
NTC_L = TL // TC
IC0 = 5632 // 128
IC1 = 5504 // 128
GW = 768
NB_G = 6
PAIRS = [[0, 1], [2, 3], [4, 5], [6, 7]]
ALL8 = [[0, 1, 2, 3, 4, 5, 6, 7]]

bf16 = ml_dtypes.bfloat16
DEBUG_OUTS = set()


class Prog:
    ENGINES = ("tensor", "vector", "scalar", "gpsimd", "sync")

    def __init__(self, nc, stack):
        self.nc = nc
        self.sems = {}
        self.cnt = {}
        self.dsems = {}
        self.dcnt = {}
        self.thunks = {e: [] for e in self.ENGINES}
        self.last_ev = {e: None for e in self.ENGINES}
        for e in self.ENGINES:
            self.sems[e] = stack.enter_context(nc.semaphore(f"sem_{e}"))
            self.dsems[e] = stack.enter_context(nc.semaphore(f"dsem_{e}"))
            self.cnt[e] = 0
            self.dcnt[e] = 0
        self.ccsem = stack.enter_context(nc.semaphore("ccsem"))
        self.cccnt = 0

    def op(self, eng, fn, deps=(), inc=1, chain=True, dma=False, cc=False):
        if cc:
            my_sem = self.ccsem
        elif dma:
            my_sem = self.dsems[eng]
        else:
            my_sem = self.sems[eng]
        seen = {}
        auto = []
        if chain and eng != "tensor" and self.last_ev[eng] is not None:
            auto = [self.last_ev[eng]]
        for ev in list(deps) + auto:
            if ev is None:
                continue
            sem, val = ev
            key = id(sem)
            if key not in seen or seen[key][1] < val:
                seen[key] = (sem, val)
        dep_list = list(seen.values())
        if cc:
            self.cccnt += inc
            n_after = self.cccnt
        elif dma:
            self.dcnt[eng] += inc
            n_after = self.dcnt[eng]
        else:
            self.cnt[eng] += inc
            n_after = self.cnt[eng]

        def thunk(e, fn=fn, dep_list=dep_list, inc=inc, my_sem=my_sem):
            for sem, val in dep_list:
                e.wait_ge(sem, val)
            inst = fn(e)
            if inc:
                if isinstance(inst, (list, tuple)):
                    inst = inst[-1]
                inst.then_inc(my_sem, inc)

        self.thunks[eng].append(thunk)
        ev = (my_sem, n_after)
        self.last_ev[eng] = ev
        return ev

    def dma(self, eng, out, in_, deps=(), **kw):
        return self.op(eng, lambda e: e.dma_start(out=out, in_=in_, **kw),
                       deps=deps, inc=16, dma=True)

    def cc(self, kind, op_, groups, ins, outs, deps=()):
        return self.op("gpsimd", lambda e: e.collective_compute(
            kind, op_, replica_groups=groups, ins=ins, outs=outs),
            deps=deps, inc=1, cc=True)

    def ig(self, out_sb, view, idx_ap, elem_off, deps=()):
        return self.op("gpsimd", lambda e: e.indirect_dma_start(
            out=out_sb, out_offset=None, in_=view,
            in_offset=bass.IndirectOffsetOnAxis(ap=idx_ap, axis=0),
            element_offset=elem_off), deps=deps, inc=16, dma=True)

    def join(self):
        evs = []
        for e in self.ENGINES:
            if self.cnt[e]:
                evs.append((self.sems[e], self.cnt[e]))
            if self.dcnt[e]:
                evs.append((self.dsems[e], self.dcnt[e]))
        if self.cccnt:
            evs.append((self.ccsem, self.cccnt))
        return evs

    def final_barrier(self):
        for e in self.ENGINES:
            if not self.thunks[e]:
                continue
            me = self

            def thunk(eng, e=e):
                if me.cnt[e]:
                    eng.wait_ge(me.sems[e], me.cnt[e])
                if me.dcnt[e]:
                    eng.wait_ge(me.dsems[e], me.dcnt[e])
                if e == "gpsimd" and me.cccnt:
                    eng.wait_ge(me.ccsem, me.cccnt)

            self.thunks[e].append(thunk)

    def finalize(self, block):
        for e in self.ENGINES:
            if not self.thunks[e]:
                continue
            thunks = self.thunks[e]

            def body(engine, thunks=thunks):
                for t in thunks:
                    t(engine)

            getattr(block, e)(body)


def emit_fused(nc, P, st):
    dram = lambda name, shape, dt_, kind=None: nc.dram_tensor(
        name, shape, dt_, kind=(kind or "Internal"))
    x0 = dram("x0", [D, TL], F32, "ExternalInput")
    x1h = dram("x1h", [D, TL], F32, "ExternalInput")
    xl = dram("xl", [D, TL], F32, "ExternalInput")
    xg = dram("xg", [1024, TF], F32, "ExternalInput")
    xpw = dram("xpw", [D, 160], F32, "ExternalInput")
    dtw = dram("dtw", [128, 1024], F32, "ExternalInput")
    dtb = dram("dtb", [128, DH], F32, "ExternalInput")
    alog = dram("alog", [128, DH * 16], F32, "ExternalInput")
    dpar = dram("dpar", [128, DH], F32, "ExternalInput")
    gwt = dram("gwt", [128, IC0 * DC * 128], F32, "ExternalInput")
    uwt = dram("uwt", [128, IC0 * DC * 128], F32, "ExternalInput")
    dwt = dram("dwt", [128, DC * IC0 * 128], F32, "ExternalInput")
    gs_i = dram("gs", [128, IC0], F32, "ExternalInput")
    us_i = dram("us", [128, IC0], F32, "ExternalInput")
    ds_i = dram("ds", [128, DC], F32, "ExternalInput")
    wq_i = dram("wq", [128, 8 * DC * 128], F32, "ExternalInput")
    wk_i = dram("wk", [128, 2 * DC * 128], F32, "ExternalInput")
    wv_i = dram("wv", [128, 2 * DC * 128], F32, "ExternalInput")
    wo_i = dram("wo", [128, DC * DC * 128], F32, "ExternalInput")
    bo_i = dram("bo", [128, DC], F32, "ExternalInput")
    cosr_i = dram("cosr", [128, TF], F32, "ExternalInput")
    sins_i = dram("sins", [128, TF], F32, "ExternalInput")
    perm_i = dram("perm", [128, 128], F32, "ExternalInput")
    id2_i = dram("id2", [128, 64], F32, "ExternalInput")
    masks_i = dram("masks", [128, 4 * 512], F32, "ExternalInput")
    onesf_i = dram("onesf", [1, 128], F32, "ExternalInput")
    gi_i = dram("gi", [D, GW], I32, "ExternalInput")
    ui_i = dram("ui", [D, GW], I32, "ExternalInput")
    di_i = dram("di", [5504, 256], I32, "ExternalInput")
    lut_i = dram("lut", [1, 16], F32, "ExternalInput")
    idxp_i = dram("idxp", [128, 1], I32, "ExternalInput")
    selh_i = dram("selh", [1, 2], F32, "ExternalInput")
    out_o = dram("out", [D, TL], F32, "ExternalOutput")

    xn_b = dram("xn_b", [1024, TF], BF16)
    delta_d = dram("delta_d", [1024, TF], F32)
    bc_d = dram("bc_d", [32, TF], F32)
    rows_d = dram("rows_d", [1, 2 * TF], F32)
    ff_d = dram("ff_d", [D, TL], F32)
    s_d = dram("s_d", [1024, TF], F32)
    ag_s = dram("ag_s", [2048, TF], F32)
    xn1_d = dram("xn1_d", [D, TL], BF16)
    ag_xn1 = dram("ag_xn1", [2 * D, TL], BF16)
    x1_d = dram("x1_d", [D, TL], F32)
    ctx_d = dram("ctx_d", [1024, TF], BF16)
    ag_ctx = dram("ag_ctx", [2048, TF], BF16)
    cg_d = dram("cg_d", [128, NB_G * DC * 128], BF16)
    cu_d = dram("cu_d", [128, NB_G * DC * 128], BF16)
    cd_d = dram("cd_d", [128, 2 * IC1 * 128], BF16)
    ag_g = dram("ag_g", [8 * 128, NB_G * DC * 128], BF16)
    ag_u = dram("ag_u", [8 * 128, NB_G * DC * 128], BF16)
    ag_d = dram("ag_d", [8 * 128, 2 * IC1 * 128], BF16)
    gwt_b = dram("gwt_b", [128, IC0 * DC * 128], BF16)
    uwt_b = dram("uwt_b", [128, IC0 * DC * 128], BF16)
    dwt_b = dram("dwt_b", [128, DC * IC0 * 128], BF16)
    wq_b = dram("wq_b", [128, 8 * DC * 128], BF16)
    wk_b = dram("wk_b", [128, 2 * DC * 128], BF16)
    wv_b = dram("wv_b", [128, 2 * DC * 128], BF16)
    wo_b = dram("wo_b", [128, DC * DC * 128], BF16)

    x0_t = x0.ap().rearrange("(c p) t -> p c t", p=128)
    x1h_t = x1h.ap().rearrange("(c p) t -> p c t", p=128)
    xl_t = xl.ap().rearrange("(c p) t -> p c t", p=128)
    xg_t = xg.ap().rearrange("(c p) t -> p c t", p=128)
    xpw_t = xpw.ap().rearrange("(c p) e -> p c e", p=128)
    xn_b_t = xn_b.ap().rearrange("(c p) t -> p c t", p=128)
    delta_t = delta_d.ap().rearrange("(c p) t -> p c t", p=128)
    s_d_t = s_d.ap().rearrange("(c p) t -> p c t", p=128)
    ff_t = ff_d.ap().rearrange("(c p) t -> p c t", p=128)
    x1_t = x1_d.ap().rearrange("(c p) t -> p c t", p=128)
    xn1_t = xn1_d.ap().rearrange("(c p) t -> p c t", p=128)
    gwt_t = gwt_b.ap().rearrange("p (ic w) -> p ic w", ic=IC0)
    uwt_t = uwt_b.ap().rearrange("p (ic w) -> p ic w", ic=IC0)
    dwt_t = dwt_b.ap().rearrange("p (hc w) -> p hc w", hc=DC)
    wq_t = wq_b.ap().rearrange("p (oc w) -> p oc w", oc=8)
    wk_t = wk_b.ap().rearrange("p (oc w) -> p oc w", oc=2)
    wv_t = wv_b.ap().rearrange("p (oc w) -> p oc w", oc=2)
    wo_t = wo_b.ap().rearrange("p (oc w) -> p oc w", oc=DC)
    gi_t = gi_i.ap().rearrange("(c p) w -> p c w", p=128)
    ui_t = ui_i.ap().rearrange("(c p) w -> p c w", p=128)
    di_t = di_i.ap().rearrange("(c p) w -> p c w", p=128)
    cg_t = cg_d.ap().rearrange("p (lc c i) -> p lc c i", lc=NB_G, c=DC)
    cu_t = cu_d.ap().rearrange("p (lc c i) -> p lc c i", lc=NB_G, c=DC)
    cd_t = cd_d.ap().rearrange("p (oc ic o) -> p oc ic o", oc=2, ic=IC1)
    xn_glob = ag_xn1.ap().rearrange("(r c p) t -> p r c t", p=128, c=DC)
    agg_v = ag_g.ap().rearrange("(r p) (lc w) -> p r lc w", p=128, lc=NB_G)
    agu_v = ag_u.ap().rearrange("(r p) (lc w) -> p r lc w", p=128, lc=NB_G)
    agd_v = ag_d.ap().rearrange("(r p) (oc w) -> p r oc w", p=128, oc=2)
    ags_flat = ag_s.ap().rearrange("r (k s) -> (r k) s", s=2048)
    agc_flat = ag_ctx.ap().rearrange("r (k s) -> (r k) s", s=2048)
    rows_flat = rows_d.ap().rearrange("r (k s) -> (r k) s", s=2048)
    out_t = out_o.ap().rearrange("(c p) t -> p c t", p=128)

    sbp = lambda stk: (lambda name, shape, dt_: stk.enter_context(nc.sbuf_tensor(name, shape, dt_)))
    psp = lambda stk: (lambda name, shape, dt_=F32: stk.enter_context(nc.psum_tensor(name, shape, dt_)))

    sb = sbp(st)
    ones_f = sb("ones_f", [1, 128], F32)
    ones_b = sb("ones_b", [128, 1], BF16)
    cst = sb("cst", [128, 2], F32)
    idxp = sb("idxp_sb", [128, 1], I32)
    rstd_l = sb("rstd_l", [1, TL], F32)
    brow_l = sb("brow_l", [1, TL], F32)
    rowh = [sb(f"rowh{i}", [1, TC], F32) for i in range(2)]
    selh = sb("selh_sb", [1, 2], F32)

    ev_of = P.dma("sync", ones_f[:, :], onesf_i.ap()[:, :])
    ev_idx = P.dma("sync", idxp[:, :], idxp_i.ap()[:, :])
    ev_selh = P.dma("sync", selh[:, :], selh_i.ap()[:, :])
    ev_ob = P.op("vector", lambda e: e.memset(ones_b[:, :], 1.0))
    P.op("vector", lambda e: e.memset(cst[:, 0:1], 1e-5))
    ev_cst = P.op("vector", lambda e: e.memset(cst[:, 1:2], 1.0))

    wcast_evs = {}
    for nm, srcw, dstw, nch in (("g", gwt, gwt_b, 8), ("u", uwt, uwt_b, 8),
                                ("d", dwt, dwt_b, 8), ("q", wq_i, wq_b, 2),
                                ("k", wk_i, wk_b, 1), ("v", wv_i, wv_b, 1),
                                ("o", wo_i, wo_b, 4)):
        cols = srcw.ap().tensor.shape[1]
        cc_ = cols // nch
        for k in range(nch):
            wcast_evs[nm] = P.dma("gpsimd", dstw.ap()[:, k * cc_:(k + 1) * cc_],
                                  srcw.ap()[:, k * cc_:(k + 1) * cc_])

    gst = st.enter_context(contextlib.ExitStack())
    gsb = sbp(gst)
    GBW = 1024
    g_idx = gsb("g_idx", [128, GBW], I32)
    g_ibf = gsb("g_ibf", [128, GBW], BF16)
    g_tmp = gsb("g_tmp", [128, GBW], BF16)
    g_acc = gsb("g_acc", [128, GBW], BF16)
    g_lut = gsb("g_lut", [128, 16], F32)
    ev_lut = P.dma("gpsimd", g_lut[:, :], lut_i.ap()[:, :].partition_broadcast(128))

    ghost_jobs = [(gi_t, cg_t, DC, GW, NB_G, 1), (ui_t, cu_t, DC, GW, NB_G, 1),
                  (di_t, cd_t, IC1, 256, 2, 4)]
    ghost_groups = []
    for j, job in enumerate(ghost_jobs):
        nch, gsz = job[2], job[5]
        ch = 0
        while ch < nch:
            ghost_groups.append((j, ch, min(gsz, nch - ch)))
            ch += gsz
    ghost_state = dict(pos=0, idx_free=None, acc_free=None)
    ghost_store_evs = {0: [], 1: [], 2: []}

    def emit_ghost_group(j, ch, g):
        src_t, dst_t, nch, w_, ntile, gsz = ghost_jobs[j]
        W = g * w_
        ev_ld = P.dma("scalar", g_idx[:, 0:W].rearrange("p (c w) -> p c w", c=g),
                      src_t[:, ch:ch + g, :], deps=[ghost_state["idx_free"]])
        ev_cv = P.op("vector", lambda e, W=W: e.tensor_copy(
            g_ibf[:, 0:W], g_idx[:, 0:W]), deps=[ev_ld])
        ghost_state["idx_free"] = ev_cv
        P.op("vector", lambda e, W=W: e.tensor_scalar(
            g_acc[:, 0:W], g_ibf[:, 0:W], 0.0, g_lut[:, 0:1], ALU.is_equal, ALU.mult),
            deps=[ev_lut, ghost_state["acc_free"]])
        for k_ in range(1, 16):
            P.op("vector", lambda e, k_=k_, W=W: e.tensor_scalar(
                g_tmp[:, 0:W], g_ibf[:, 0:W], float(k_), g_lut[:, k_:k_ + 1],
                ALU.is_equal, ALU.mult))
            P.op("vector", lambda e, W=W: e.tensor_add(
                g_acc[:, 0:W], g_acc[:, 0:W], g_tmp[:, 0:W]))
        ev_done = P.last_ev["vector"]
        for ci in range(g):
            ev_st = P.dma("scalar", dst_t[:, :, ch + ci, :],
                          g_acc[:, ci * w_:(ci + 1) * w_].rearrange("p (l i) -> p l i", l=ntile),
                          deps=[ev_done] if ci == 0 else ())
            ghost_store_evs[j].append(ev_st)
        ghost_state["acc_free"] = ev_st

    def next_ghost(k):
        while k > 0 and ghost_state["pos"] < len(ghost_groups):
            j, ch, g = ghost_groups[ghost_state["pos"]]
            emit_ghost_group(j, ch, g)
            ghost_state["pos"] += 1
            k -= 1

    ast = st.enter_context(contextlib.ExitStack())
    asb, aps = sbp(ast), psp(ast)
    a_xt = asb("a_xt", [128, DC * TC], F32)
    a_xbf = asb("a_xbf", [128, DC * TC], BF16)
    a_sq = asb("a_sq", [128, DC * TC], BF16)
    a_xn = asb("a_xn", [128, DC * TC], BF16)
    a_xg = asb("a_xg", [128, DH * TC], F32)
    a_xns = asb("a_xns", [128, DH * TC], BF16)
    a_dt = asb("a_dt", [128, DH * TC], F32)
    a_rows = asb("a_rows", [1, 5 * TC], F32)
    a_dt16 = asb("a_dt16", [128, TC], BF16)
    a_bcf = asb("a_bcf", [32, TC], F32)
    a_zf = asb("a_zf", [128, TC], F32)
    a_az = asb("a_az", [128, TC], F32)
    a_el = asb("a_el", [128, TC], F32)
    a_rz = asb("a_rz", [128, TC], F32)
    a_xpw = asb("a_xpw", [128, DC * 160], BF16)
    a_dtw = asb("a_dtw", [128, 1024], BF16)
    a_dtb = asb("a_dtb", [128, DH], F32)
    p_mean = aps("p_mean", [1, TC])
    p_msq = aps("p_msq", [1, TC])
    p_a = aps("p_a", [128, TC])
    p_b = aps("p_b", [128, TC])
    p_dt = aps("p_dt", [128, TC])
    p_bc = aps("p_bc", [32, TC])
    p_z = aps("p_z", [128, TC])

    ev_w1 = P.dma("gpsimd", a_xpw[:, :].rearrange("p (c e) -> p c e", e=160), xpw_t)
    ev_w2 = P.dma("gpsimd", a_dtw[:, :], dtw.ap()[:, :])
    ev_w3 = P.dma("sync", a_dtb[:, :], dtb.ap()[:, :])

    ev_prev_stats = None
    ev_prev_bcast = None
    ev_apply_last = None
    ev_sq = ev_xc = None
    ev_xg_used = None
    ev_e2 = None
    ev_evdt = None
    ev_z_read = None
    ev_rz = None
    ev_dl = None
    ev_dstore = None
    ev_xns_store = None
    ev_bc_store = None
    rows_store_evs = []
    ev_brow_prev = ev_r3_prev = ev_rg1_prev = ev_rg2_prev = None
    delta_store_evs = []
    xns_store_evs = []
    bc_store_evs = []

    R = lambda i: a_rows[0:1, i * TC:(i + 1) * TC]
    for tc in range(NTC_G):
        tsl = slice(tc * TC, (tc + 1) * TC)
        src = x0_t if tc < NTC_G // 2 else x1h_t
        lsl = slice((tc % 4) * TC, (tc % 4) * TC + TC)
        ev_x = P.dma("sync", a_xt[:, :].rearrange("p (c t) -> p c t", t=TC),
                     src[:, :, lsl], deps=[ev_sq, ev_xc, ev_apply_last])
        ev_xc = P.op("vector", lambda e: e.tensor_copy(a_xbf[:, :], a_xt[:, :]), deps=[ev_x])
        ev_sq = P.op("scalar", lambda e: e.activation(a_sq[:, :], a_xt[:, :], AF.Square), deps=[ev_x])
        for c in range(DC):
            last = c == DC - 1
            P.op("tensor", lambda e, c=c, last=last: e.matmul(
                p_mean[:, :], ones_b[:, :], a_xbf[:, c * TC:(c + 1) * TC],
                start=(c == 0), stop=last),
                deps=[ev_xc, ev_ob, ev_prev_stats] if c == 0 else (), inc=1 if last else 0)
        ev_mm_mean = P.last_ev["tensor"]
        for c in range(DC):
            last = c == DC - 1
            P.op("tensor", lambda e, c=c, last=last: e.matmul(
                p_msq[:, :], ones_b[:, :], a_sq[:, c * TC:(c + 1) * TC],
                start=(c == 0), stop=last),
                deps=[ev_sq, ev_prev_stats] if c == 0 else (), inc=1 if last else 0)
        ev_mm_msq = P.last_ev["tensor"]
        ev_r0 = P.op("scalar", lambda e: e.activation(R(0), p_mean[0:1, :], AF.Copy, scale=1.0 / D),
                     deps=[ev_mm_mean, ev_brow_prev])
        ev_r1 = P.op("scalar", lambda e: e.activation(R(1), p_msq[0:1, :], AF.Copy, scale=1.0 / D),
                     deps=[ev_mm_msq, ev_r3_prev])
        ev_r2 = P.op("scalar", lambda e: e.activation(R(2), R(0), AF.Square), deps=[ev_r0])
        ev_r3 = P.op("vector", lambda e: e.tensor_sub(R(2), R(1), R(2)), deps=[ev_r1, ev_r2])
        ev_r3_prev = ev_r3
        ev_r4 = P.op("scalar", lambda e: e.activation(R(3), R(2), AF.Ln, bias=cst[0:1, 0:1]),
                     deps=[ev_r3, ev_cst, ev_rg1_prev])
        ev_rstd = P.op("scalar", lambda e: e.activation(R(3), R(3), AF.Exp, scale=-0.5), deps=[ev_r4])
        ev_brow = P.op("vector", lambda e: e.tensor_mul(R(4), R(0), R(3)), deps=[ev_rstd, ev_rg2_prev])
        ev_brow_prev = ev_brow
        ev_prev_stats = ev_r1
        ev_rg1 = P.dma("scalar", rows_d.ap()[0:1, tsl], R(3), deps=[ev_rstd])
        ev_rg2 = P.dma("scalar", rows_d.ap()[0:1, TF + tc * TC:TF + (tc + 1) * TC], R(4), deps=[ev_brow])
        ev_rg1_prev, ev_rg2_prev = ev_rg1, ev_rg2
        rows_store_evs.append(ev_rg2)
        bd = [ev_rstd, ev_brow, ev_of, ev_prev_bcast]
        ev_ba = P.op("tensor", lambda e: e.matmul(p_a[:, :], ones_f[:, :], R(3), start=True, stop=True), deps=bd)
        ev_bb = P.op("tensor", lambda e: e.matmul(p_b[:, :], ones_f[:, :], R(4), start=True, stop=True), deps=bd)
        dep0 = [ev_ba, ev_bb, ev_x]
        ev_apply_evs = []
        for c in range(DC):
            csl = slice(c * TC, (c + 1) * TC)
            e1 = P.op("vector", lambda e, csl=csl: e.tensor_mul(a_el[:, :], a_xt[:, csl], p_a[:, :]),
                      deps=dep0 if c == 0 else [ev_dl])
            e2 = P.op("vector", lambda e, csl=csl: e.tensor_sub(a_xn[:, csl], a_el[:, :], p_b[:, :]))
            ev_apply_evs.append(e2)
        ev_apply_full = P.last_ev["vector"]
        ev_xgl = P.dma("sync", a_xg[:, :].rearrange("p (c t) -> p c t", t=TC),
                       xg_t[:, :, tsl], deps=[ev_xg_used])
        for c in range(DH):
            csl = slice(c * TC, (c + 1) * TC)
            e1 = P.op("vector", lambda e, csl=csl: e.tensor_mul(a_el[:, :], a_xg[:, csl], p_a[:, :]),
                      deps=[ev_xgl, ev_xns_store])
            e2 = P.op("vector", lambda e, csl=csl: e.tensor_sub(a_xns[:, csl], a_el[:, :], p_b[:, :]))
        ev_xg_used = P.last_ev["vector"]
        ev_apply_last = ev_xg_used
        ev_prev_bcast = ev_xg_used
        ev_xns_store = P.dma("scalar", xn_b_t[:, :, tsl],
                             a_xns[:, :].rearrange("p (c t) -> p c t", t=TC),
                             deps=[ev_xg_used])
        xns_store_evs.append(ev_xns_store)
        for c in range(DC):
            last = c == DC - 1
            P.op("tensor", lambda e, c=c, last=last: e.matmul(
                p_dt[:, :], a_xpw[:, c * 160:c * 160 + 128], a_xn[:, c * TC:(c + 1) * TC],
                start=(c == 0), stop=last),
                deps=[ev_apply_full, ev_w1, ev_evdt, ev_e2] if c == 0 else (), inc=1 if last else 0)
        ev_mm_dt = P.last_ev["tensor"]
        for c in range(DC):
            last = c == DC - 1
            P.op("tensor", lambda e, c=c, last=last: e.matmul(
                p_bc[:, :], a_xpw[:, c * 160 + 128:(c + 1) * 160], a_xn[:, c * TC:(c + 1) * TC],
                start=(c == 0), stop=last), deps=(), inc=1 if last else 0)
        ev_mm_bc = P.last_ev["tensor"]
        ev_e2 = P.op("vector", lambda e: e.tensor_copy(a_dt16[:, :], p_dt[:, :]),
                     deps=[ev_mm_dt])
        ev_e3 = P.op("vector", lambda e: e.tensor_copy(a_bcf[:, :], p_bc[:, :]),
                     deps=[ev_mm_bc, ev_bc_store])
        ev_evdt = ev_e3
        ev_bc_store = P.dma("scalar", bc_d.ap()[:, tsl], a_bcf[:, :], deps=[ev_e3])
        bc_store_evs.append(ev_bc_store)
        ev_dl_all = []
        for dci in range(DH):
            ev_mm_z = P.op("tensor", lambda e, dci=dci: e.matmul(
                p_z[:, :], a_dtw[:, dci * 128:(dci + 1) * 128], a_dt16[:, :],
                start=True, stop=True), deps=[ev_e2, ev_w2, ev_z_read], inc=1)
            ev_zf = P.op("scalar", lambda e, dci=dci: e.activation(
                a_zf[:, :], p_z[:, :], AF.Identity, bias=a_dtb[:, dci:dci + 1]),
                deps=[ev_mm_z, ev_w3, ev_rz])
            ev_z_read = ev_zf
            ev_az = P.op("scalar", lambda e: e.activation(a_az[:, :], a_zf[:, :], AF.Abs), deps=[ev_zf])
            ev_el2 = P.op("scalar", lambda e: e.activation(a_el[:, :], a_az[:, :], AF.Exp, scale=-1.0),
                          deps=[ev_az, ev_dl, P.last_ev["vector"]])
            ev_l1 = P.op("scalar", lambda e: e.activation(a_el[:, :], a_el[:, :], AF.Ln, bias=cst[:, 1:2]), deps=[ev_el2])
            ev_rz = P.op("vector", lambda e: e.tensor_scalar(a_rz[:, :], a_zf[:, :], 0.0, None, ALU.max), deps=[ev_zf])
            ev_dl = P.op("vector", lambda e, dci=dci: e.tensor_add(
                a_dt[:, dci * TC:(dci + 1) * TC], a_rz[:, :], a_el[:, :]),
                deps=[ev_rz, ev_l1] + ([ev_dstore] if dci == 0 else []))
            ev_dl_all.append(ev_dl)
        ev_dstore = P.dma("scalar", delta_t[:, :, tsl],
                          a_dt[:, :].rearrange("p (c t) -> p c t", t=TC),
                          deps=[ev_dl_all[-1]])
        delta_store_evs.append(ev_dstore)
    ev_rows_store = rows_store_evs[-1]

    ev_endA = P.join()
    ast.close()

    evs_rg = []
    ev_bl_prev = None
    for which, dstrow in ((0, rstd_l), (1, brow_l)):
        base = which * TF
        for q in range(TL // TC):
            qsl = slice(q * TC, (q + 1) * TC)
            ev_rh0 = P.dma("sync", rowh[0][0:1, :], rows_d.ap()[0:1, base + q * TC:base + (q + 1) * TC],
                           deps=[ev_rows_store, ev_bl_prev])
            ev_rh1 = P.dma("sync", rowh[1][0:1, :], rows_d.ap()[0:1, base + TL + q * TC:base + TL + (q + 1) * TC])
            e1 = P.op("vector", lambda e, dstrow=dstrow, qsl=qsl: e.tensor_scalar(
                dstrow[0:1, qsl], rowh[0][0:1, :], selh[0:1, 0:1], None, ALU.mult),
                deps=[ev_rh0, ev_selh])
            e2 = P.op("vector", lambda e: e.tensor_scalar(
                rowh[1][0:1, :], rowh[1][0:1, :], selh[0:1, 1:2], None, ALU.mult),
                deps=[ev_rh1, ev_selh])
            e3 = P.op("vector", lambda e, dstrow=dstrow, qsl=qsl: e.tensor_add(
                dstrow[0:1, qsl], dstrow[0:1, qsl], rowh[1][0:1, :]), deps=[e1, e2])
            ev_bl_prev = e3
        evs_rg.append(ev_bl_prev)
    ev_rows_gather = evs_rg[0]
    ev_brow_copy = evs_rg[1]

    bst = st.enter_context(contextlib.ExitStack())
    bsb = sbp(bst)
    NG = 2
    DPG = DH // NG
    b_dlt = [bsb(f"b_dlt{i}", [128, DPG * TC], F32) for i in range(2)]
    b_xnt = [bsb(f"b_xnt{i}", [128, DPG * TC], BF16) for i in range(2)]
    b_ut = bsb("b_ut", [128, DPG * TC], F32)
    b_yt = bsb("b_yt", [128, DPG * TC], F32)
    bcb3 = [bsb(f"bcb3_{i}", [128, TC], F32) for i in range(3)]
    bcc3 = [bsb(f"bcc3_{i}", [128, TC], F32) for i in range(2)]
    b_at = [bsb(f"b_at{i}", [128, TC], F32) for i in range(2)]
    b_bbt = bsb("b_bbt", [128, TC], F32)
    b_ht = [bsb(f"b_ht{i}", [128, TC], F32) for i in range(2)]
    b_tmp = bsb("b_tmp", [128, TC], F32)
    b_A2 = bsb("b_A2", [128, DH * 16], F32)
    b_Dp = bsb("b_Dp", [128, DH], F32)
    b_carry = bsb("b_carry", [128, DH * 16], F32)

    ev_al = P.dma("sync", b_A2[:, :], alog.ap()[:, :], deps=ev_endA)
    ev_dp = P.dma("sync", b_Dp[:, :], dpar.ap()[:, :])
    ev_a1 = P.op("scalar", lambda e: e.activation(b_A2[:, :], b_A2[:, :], AF.Exp), deps=[ev_al] + ev_endA)
    ev_a2 = P.op("vector", lambda e: e.tensor_scalar(b_A2[:, :], b_A2[:, :], -1.0, None, ALU.mult),
                 deps=[ev_a1] + ev_endA)
    ev_cinit = P.op("vector", lambda e: e.memset(b_carry[:, :], 0.0))

    b_state = dict(ev_sstore=None, s_store_evs=[],
                   chunk_store=[None, None], bc_used=[None, None, None],
                   bcc_used=[None, None], si=0,
                   scan_p=[None, None], hc_p=[None, None])

    def emit_B_chunk(dcg, tc):
        k = dcg * NTC_G + tc
        cb = k % 2
        dlt, xnt = b_dlt[cb], b_xnt[cb]
        tsl = slice(tc * TC, (tc + 1) * TC)
        dsl = slice(dcg * DPG, (dcg + 1) * DPG)
        ev_dld = P.dma("sync", dlt[:, :].rearrange("p (c t) -> p c t", t=TC),
                       delta_t[:, dsl, tsl],
                       deps=[b_state["chunk_store"][cb], delta_store_evs[tc]])
        ev_xld = P.dma("sync", xnt[:, :].rearrange("p (c t) -> p c t", t=TC),
                       xn_b_t[:, dsl, tsl],
                       deps=[b_state["chunk_store"][cb], xns_store_evs[tc]])
        ev_yinit = []
        for j in range(DPG):
            csl = slice(j * TC, (j + 1) * TC)
            dci = dcg * DPG + j
            P.op("vector", lambda e, csl=csl, dlt=dlt, xnt=xnt: e.tensor_mul(
                dlt[:, csl] if False else b_ut[:, csl], dlt[:, csl], xnt[:, csl]),
                 deps=[ev_dld, ev_xld, b_state["ev_sstore"]])
            ev_yi = P.op("vector", lambda e, csl=csl, dci=dci, xnt=xnt: e.tensor_scalar(
                b_yt[:, csl], xnt[:, csl], b_Dp[:, dci:dci + 1], None, ALU.mult), deps=[ev_dp])
            ev_yinit.append(ev_yi)
        ev_y = None
        for n in range(16):
            nb = n % 3
            bcB, bcC = bcb3[nb], bcc3[n % 2]
            ev_bB = P.dma("sync", bcB[:, :], bc_d.ap()[n:n + 1, tsl].partition_broadcast(128),
                          deps=[b_state["bc_used"][nb], bc_store_evs[tc]])
            ev_bC = P.dma("sync", bcC[:, :], bc_d.ap()[16 + n:17 + n, tsl].partition_broadcast(128),
                          deps=[b_state["bcc_used"][n % 2]])
            ev_bb = None
            ev_hc = None
            for j in range(DPG):
                csl = slice(j * TC, (j + 1) * TC)
                dci = dcg * DPG + j
                idx = dci * 16 + n
                p = b_state["si"] % 2
                ev_a = P.op("scalar", lambda e, csl=csl, idx=idx, dlt=dlt, p=p: e.activation(
                    b_at[p][:, :], dlt[:, csl], AF.Exp, scale=b_A2[:, idx:idx + 1]),
                    deps=[ev_a2, ev_dld, b_state["scan_p"][p]])
                ev_bb = P.op("vector", lambda e, csl=csl, bcB=bcB: e.tensor_mul(
                    b_bbt[:, :], b_ut[:, csl], bcB[:, :]), deps=[ev_bB])
                ev_scan = P.op("vector", lambda e, idx=idx, p=p: e.tensor_tensor_scan(
                    b_ht[p][:, :], b_at[p][:, :], b_bbt[:, :], b_carry[:, idx:idx + 1],
                    ALU.mult, ALU.add),
                    deps=[ev_a, ev_cinit, b_state["hc_p"][p]])
                b_state["scan_p"][p] = ev_scan
                P.op("vector", lambda e, idx=idx, p=p: e.tensor_copy(
                    b_carry[:, idx:idx + 1], b_ht[p][:, TC - 1:TC]))
                ev_hc = P.op("gpsimd", lambda e, bcC=bcC, p=p: e.tensor_mul(
                    b_tmp[:, :], b_ht[p][:, :], bcC[:, :]), deps=[ev_scan, ev_bC])
                b_state["hc_p"][p] = ev_hc
                ev_y = P.op("gpsimd", lambda e, csl=csl: e.tensor_add(
                    b_yt[:, csl], b_yt[:, csl], b_tmp[:, :]),
                    deps=[ev_yinit[j]] if n == 0 else ())
                b_state["si"] += 1
            b_state["bc_used"][nb] = ev_bb
            b_state["bcc_used"][n % 2] = ev_hc
        for j in range(DPG):
            csl = slice(j * TC, (j + 1) * TC)
            P.op("vector", lambda e, csl=csl: e.tensor_scalar(b_yt[:, csl], b_yt[:, csl], 0.5, None, ALU.mult),
                 deps=[ev_y] if j == 0 else ())
        ev_st = P.dma("sync", s_d_t[:, dsl, tsl], b_yt[:, :].rearrange("p (c t) -> p c t", t=TC),
                      deps=[P.last_ev["vector"]])
        b_state["ev_sstore"] = ev_st
        b_state["chunk_store"][cb] = ev_st
        b_state["s_store_evs"].append(ev_st)

    cst_k = st.enter_context(contextlib.ExitStack())
    csb, cps = sbp(cst_k), psp(cst_k)
    c_xc = [csb(f"c_xc{i}", [128, TC], F32) for i in range(2)]
    c_xn = csb("c_xn", [128, DC * TC], BF16)
    c_hdn = csb("c_hdn", [128, IC0 * TC], BF16)
    c_gw = csb("c_gw", [128, DC * 128], BF16)
    c_uw = csb("c_uw", [128, DC * 128], BF16)
    c_dw = csb("c_dw", [128, IC0 * 128], BF16)
    c_gs = csb("c_gs", [128, IC0], F32)
    c_us = csb("c_us", [128, IC0], F32)
    c_ds = csb("c_ds", [128, DC], F32)
    c_gg = csb("c_gg", [128, TC], F32)
    c_uu = csb("c_uu", [128, TC], F32)
    c_th = csb("c_th", [128, TC], F32)
    c_t1 = csb("c_t1", [128, TC], F32)
    c_ff = csb("c_ff", [128, TC], F32)
    p_cg = cps("p_cg", [128, TC])
    p_cu = cps("p_cu", [128, TC])
    p_cd = cps("p_cd", [128, TC])
    p_ca = cps("p_ca", [128, TC])
    p_cb = cps("p_cb", [128, TC])


    ev_gs = P.dma("scalar", c_gs[:, :], gs_i.ap()[:, :], deps=ev_endA)
    ev_us = P.dma("scalar", c_us[:, :], us_i.ap()[:, :])
    ev_ds = P.dma("scalar", c_ds[:, :], ds_i.ap()[:, :])
    ev_us2 = P.op("vector", lambda e: e.tensor_scalar(c_us[:, :], c_us[:, :], 0.5, None, ALU.mult),
                  deps=[ev_us] + ev_endA)

    c_state = dict(ev_gw_used=None, ev_uw_used=None, ev_dw_used=None,
                   ev_evg=None, ev_evu=None, ev_th_r=None, ev_gg_r=None,
                   ev_ff_r=None, ev_hdn_used=None, ev_ffst=None,
                   ev_xn_used=None, ff_store_evs=[])

    def emit_Cfwd_tc(tc):
        tsl = slice(tc * TC, (tc + 1) * TC)
        ev_pa = P.op("tensor", lambda e, tsl=tsl: e.matmul(
            p_ca[:, :], ones_f[:, :], rstd_l[0:1, tsl], start=True, stop=True),
            deps=[ev_rows_gather, ev_of, c_state["ev_xn_used"]])
        ev_pb = P.op("tensor", lambda e, tsl=tsl: e.matmul(
            p_cb[:, :], ones_f[:, :], brow_l[0:1, tsl], start=True, stop=True),
            deps=[ev_brow_copy, c_state["ev_xn_used"]])
        ev_applies = []
        for c in range(DC):
            bi = c % 2
            ev_xcl = P.dma("scalar", c_xc[bi][:, :], xl_t[:, c, tsl],
                           deps=[ev_applies[-2] if c >= 2 else None])
            e1 = P.op("vector", lambda e, bi=bi: e.tensor_mul(c_ff[:, :], c_xc[bi][:, :], p_ca[:, :]),
                      deps=[ev_xcl, ev_pa, ev_pb, c_state["ev_ff_r"]])
            e2 = P.op("vector", lambda e, c=c: e.tensor_sub(c_xn[:, c * TC:(c + 1) * TC], c_ff[:, :], p_cb[:, :]))
            ev_applies.append(e2)
            c_state["ev_ff_r"] = e2
        ev_xn_ready = ev_applies[-1]
        for ic in range(IC0):
            ev_gwl = P.dma("scalar", c_gw[:, :], gwt_t[:, ic, :],
                           deps=[c_state["ev_gw_used"], wcast_evs["g"]])
            ev_uwl = P.dma("scalar", c_uw[:, :], uwt_t[:, ic, :],
                           deps=[c_state["ev_uw_used"], wcast_evs["u"]])
            for c in range(DC):
                last = c == DC - 1
                P.op("tensor", lambda e, c=c, last=last: e.matmul(
                    p_cg[:, :], c_gw[:, c * 128:(c + 1) * 128], c_xn[:, c * TC:(c + 1) * TC],
                    start=(c == 0), stop=last),
                    deps=[ev_gwl, ev_xn_ready, c_state["ev_evg"]] if c == 0 else (),
                    inc=1 if last else 0)
            ev_mm_g = P.last_ev["tensor"]
            c_state["ev_gw_used"] = ev_mm_g
            for c in range(DC):
                last = c == DC - 1
                P.op("tensor", lambda e, c=c, last=last: e.matmul(
                    p_cu[:, :], c_uw[:, c * 128:(c + 1) * 128], c_xn[:, c * TC:(c + 1) * TC],
                    start=(c == 0), stop=last),
                    deps=[ev_uwl, ev_xn_ready, c_state["ev_evu"]] if c == 0 else (),
                    inc=1 if last else 0)
            ev_mm_u = P.last_ev["tensor"]
            c_state["ev_uw_used"] = ev_mm_u
            ev_evg = P.op("vector", lambda e, ic=ic: e.tensor_scalar(
                c_gg[:, :], p_cg[:, :], c_gs[:, ic:ic + 1], None, ALU.mult),
                deps=[ev_mm_g, ev_gs, c_state["ev_gg_r"]])
            c_state["ev_evg"] = ev_evg
            ev_evu = P.op("vector", lambda e, ic=ic: e.tensor_scalar(
                c_uu[:, :], p_cu[:, :], c_us[:, ic:ic + 1], None, ALU.mult),
                deps=[ev_mm_u, ev_us2])
            c_state["ev_evu"] = ev_evu
            ev_th = P.op("scalar", lambda e: e.activation(c_th[:, :], c_gg[:, :], AF.Tanh, scale=0.5),
                         deps=[ev_evg, c_state["ev_th_r"]])
            c_state["ev_gg_r"] = ev_th
            ev_t1 = P.op("vector", lambda e: e.scalar_tensor_tensor(
                c_t1[:, :], c_th[:, :], 1.0, c_gg[:, :], ALU.add, ALU.mult), deps=[ev_th])
            c_state["ev_th_r"] = ev_t1
            ev_hdn = P.op("vector", lambda e, ic=ic: e.tensor_mul(
                c_hdn[:, ic * TC:(ic + 1) * TC], c_t1[:, :], c_uu[:, :]),
                deps=[ev_evu, c_state["ev_hdn_used"]])
        for hc in range(DC):
            ev_dwl = P.dma("scalar", c_dw[:, :], dwt_t[:, hc, :],
                           deps=[c_state["ev_dw_used"], wcast_evs["d"]])
            for ic in range(IC0):
                last = ic == IC0 - 1
                P.op("tensor", lambda e, ic=ic, last=last: e.matmul(
                    p_cd[:, :], c_dw[:, ic * 128:(ic + 1) * 128], c_hdn[:, ic * TC:(ic + 1) * TC],
                    start=(ic == 0), stop=last),
                    deps=[ev_dwl, P.last_ev["vector"]] if ic == 0 else (), inc=1 if last else 0)
            ev_mm_d = P.last_ev["tensor"]
            c_state["ev_dw_used"] = ev_mm_d
            if hc == DC - 1:
                c_state["ev_hdn_used"] = ev_mm_d
            ev_ff = P.op("vector", lambda e, hc=hc: e.tensor_scalar(
                c_ff[:, :], p_cd[:, :], c_ds[:, hc:hc + 1], None, ALU.mult),
                deps=[ev_mm_d, ev_ds, c_state["ev_ff_r"], c_state["ev_ffst"]])
            c_state["ev_ff_r"] = ev_ff
            ev_ffst = P.dma("scalar", ff_t[:, hc, tsl], c_ff[:, :], deps=[ev_ff])
            c_state["ev_ffst"] = ev_ffst
            c_state["ff_store_evs"].append(ev_ffst)
        c_state["ev_xn_used"] = P.last_ev["tensor"]

    for dcg in range(NG):
        for tc in range(NTC_G):
            emit_B_chunk(dcg, tc)
    for tc2 in range(NTC_L):
        emit_Cfwd_tc(tc2)

    ev_ag_s = P.cc("AllGather", ALU.bypass, PAIRS,
                   [s_d.ap().opt()], [ag_s.ap().opt()], deps=b_state["s_store_evs"])


    c_sg = csb("c_sg", [128, TC], F32)
    c_g2 = csb("c_g2", [128, TC], F32)
    c_xs = csb("c_xs", [128, TC], F32)
    c_x1 = csb("c_x1", [128, TC], F32)
    c_x1b = csb("c_x1b", [128, TC], BF16)
    c_sqb = csb("c_sqb", [128, TC], BF16)
    c_xn1 = csb("c_xn1", [128, TC], BF16)
    c_rows = csb("c_rows", [1, 4 * TC], F32)
    gpst = st.enter_context(contextlib.ExitStack())
    p_m1 = gpst.enter_context(nc.psum_tensor("p_m1", [1, TC], F32))
    p_m2 = gpst.enter_context(nc.psum_tensor("p_m2", [1, TC], F32))
    R1 = lambda i: c_rows[0:1, i * TC:(i + 1) * TC]

    ev_sg_free = None
    ev_brow1_prev = ev_q3_prev = ev_pa1_prev = ev_pb1_prev = None
    ev_x1_used = None
    ev_x1b_used = None
    ev_sqb_used = None
    ev_ln1_apply = None
    ev_th2 = None
    ev_xn1st = None
    x1_store_evs = []
    xn1_store_evs = []
    prev_stats1 = None
    for tc in range(NTC_L):
        tsl = slice(tc * TC, (tc + 1) * TC)
        ev_x1_stores = []
        for hc in range(DC):
            ev_ffl = P.dma("scalar", c_ff[:, :], ff_t[:, hc, tsl],
                           deps=[c_state["ev_ff_r"], c_state["ff_store_evs"][-1]])
            c_state["ev_ff_r"] = None
            ev_th2 = P.op("scalar", lambda e: e.activation(c_th[:, :], c_ff[:, :], AF.Tanh, scale=0.5),
                          deps=[ev_ffl, c_state["ev_th_r"]])
            ev_g2 = P.op("vector", lambda e: e.scalar_tensor_tensor(
                c_g2[:, :], c_th[:, :], 1.0, c_ff[:, :], ALU.add, ALU.mult), deps=[ev_th2])
            c_state["ev_th_r"] = ev_g2
            c_state["ev_ff_r"] = ev_g2
            ev_sgl = P.ig(c_sg[:, :], ags_flat, idxp[:, :],
                          hc * 128 * TF + tc * TC, deps=[ev_ag_s, ev_sg_free, ev_idx])
            ev_sg2 = P.op("vector", lambda e: e.tensor_mul(c_g2[:, :], c_g2[:, :], c_sg[:, :]),
                          deps=[ev_sgl])
            ev_sg_free = ev_sg2
            ev_xsl = P.dma("scalar", c_xs[:, :], xl_t[:, hc, tsl], deps=[P.last_ev["vector"]])
            ev_x1 = P.op("vector", lambda e: e.tensor_add(c_x1[:, :], c_xs[:, :], c_g2[:, :]),
                         deps=[ev_xsl, ev_x1_used])
            ev_x1st = P.dma("scalar", x1_t[:, hc, tsl], c_x1[:, :], deps=[ev_x1])
            ev_x1_stores.append(ev_x1st)
            ev_x1b = P.op("vector", lambda e: e.tensor_copy(c_x1b[:, :], c_x1[:, :]),
                          deps=[ev_x1, ev_x1b_used])
            ev_sqb = P.op("scalar", lambda e: e.activation(c_sqb[:, :], c_x1[:, :], AF.Square),
                          deps=[ev_x1, ev_sqb_used])
            ev_x1_used = ev_sqb
            P.op("tensor", lambda e, hc=hc: e.matmul(
                p_m1[:, :], ones_b[:, :], c_x1b[:, :], start=(hc == 0), stop=(hc == DC - 1)),
                deps=[ev_x1b, prev_stats1] if hc == 0 else [ev_x1b],
                inc=1 if hc == DC - 1 else 0)
            ev_mmm1 = P.last_ev["tensor"]
            P.op("tensor", lambda e, hc=hc: e.matmul(
                p_m2[:, :], ones_b[:, :], c_sqb[:, :], start=(hc == 0), stop=(hc == DC - 1)),
                deps=[ev_sqb, prev_stats1] if hc == 0 else [ev_sqb],
                inc=1 if hc == DC - 1 else 0)
            ev_mmm2 = P.last_ev["tensor"]
            ev_x1b_used = ev_mmm1
            ev_sqb_used = ev_mmm2
        ev_q0 = P.op("scalar", lambda e: e.activation(R1(0), p_m1[0:1, :], AF.Copy, scale=1.0 / D),
                     deps=[ev_mmm1, ev_brow1_prev])
        ev_q1 = P.op("scalar", lambda e: e.activation(R1(1), p_m2[0:1, :], AF.Copy, scale=1.0 / D),
                     deps=[ev_mmm2, ev_q3_prev, ev_pb1_prev])
        ev_q2 = P.op("scalar", lambda e: e.activation(R1(2), R1(0), AF.Square), deps=[ev_q0])
        ev_q3 = P.op("vector", lambda e: e.tensor_sub(R1(2), R1(1), R1(2)), deps=[ev_q1, ev_q2])
        ev_q3_prev = ev_q3
        ev_q4 = P.op("scalar", lambda e: e.activation(R1(3), R1(2), AF.Ln, bias=cst[0:1, 0:1]),
                     deps=[ev_q3, ev_pa1_prev])
        ev_rstd1 = P.op("scalar", lambda e: e.activation(R1(3), R1(3), AF.Exp, scale=-0.5), deps=[ev_q4])
        ev_brow1 = P.op("vector", lambda e: e.tensor_mul(R1(1), R1(0), R1(3)),
                        deps=[ev_rstd1, ev_pb1_prev, ev_q3])
        ev_brow1_prev = ev_brow1
        prev_stats1 = ev_q1
        ev_pa1 = P.op("tensor", lambda e: e.matmul(p_ca[:, :], ones_f[:, :], R1(3), start=True, stop=True),
                      deps=[ev_rstd1, ev_brow1, c_state["ev_xn_used"], ev_ln1_apply])
        ev_pb1 = P.op("tensor", lambda e: e.matmul(p_cb[:, :], ones_f[:, :], R1(1), start=True, stop=True),
                      deps=[ev_rstd1, ev_brow1, ev_ln1_apply])
        ev_pa1_prev, ev_pb1_prev = ev_pa1, ev_pb1
        for c in range(DC):
            bi = c % 2
            ev_x1l = P.dma("scalar", c_xc[bi][:, :], x1_t[:, c, tsl],
                           deps=[ev_x1_stores[c]] + ([P.last_ev["vector"]] if c >= 2 else []))
            e1 = P.op("vector", lambda e, bi=bi: e.tensor_mul(c_g2[:, :], c_xc[bi][:, :], p_ca[:, :]),
                      deps=[ev_x1l, ev_pa1, ev_pb1])
            e2 = P.op("vector", lambda e: e.tensor_sub(c_xn1[:, :], c_g2[:, :], p_cb[:, :]),
                      deps=[ev_xn1st])
            ev_ln1_apply = e2
            ev_xn1st = P.dma("scalar", xn1_t[:, c, tsl], c_xn1[:, :], deps=[e2])
            xn1_store_evs.append(ev_xn1st)
        x1_store_evs.extend(ev_x1_stores)

    ev_ag_xn1 = P.cc("AllGather", ALU.bypass, PAIRS,
                     [xn1_d.ap().opt()], [ag_xn1.ap().opt()], deps=xn1_store_evs)

    ev_endBC = P.join()
    gpst.close()
    cst_k.close()
    bst.close()

    dst = st.enter_context(contextlib.ExitStack())
    dsb, dps = sbp(dst), psp(dst)
    NOC_Q = 8
    d_xn = dsb("d_xn", [128, DC * TC], BF16)
    d_w = [dsb(f"d_w{i}", [128, DC * 128], BF16) for i in range(2)]
    q_sb = dsb("q_sb", [128, NOC_Q * TF], BF16)
    k_sb = dsb("k_sb", [128, 4 * TF], BF16)
    v_sb = dsb("v_sb", [128, 4 * 32 * 65], BF16)
    v_tmp = dsb("v_tmp", [128, TC], BF16)
    qf_bf = dsb("qf_bf", [128, TC], BF16)
    krope = dsb("krope", [128, TC], BF16)
    d_At = dsb("d_At", [128, TC], F32)
    cos_sb = dsb("cos_sb", [128, TC], F32)
    sin_sb = dsb("sin_sb", [128, TC], F32)
    perm_sb = dsb("perm_sb", [128, 128], BF16)
    id2_sb = dsb("id2_sb", [128, 64], BF16)
    masks_sb = dsb("masks_sb", [128, 4 * 512], BF16)
    q2t = dsb("q2t", [128, TC], BF16)
    qq = dsb("qq", [1, 32], F32)
    d_brow = dsb("d_brow", [1, 16], F32)
    biasT = dsb("biasT", [128, 16], F32)
    pT = [dsb(f"pT{i}", [128, TC], BF16) for i in range(4)]
    lrow = dsb("lrow", [1, TC], F32)
    ctxf = dsb("ctxf", [64, TC], F32)
    outbf = dsb("outbf", [64, TC], BF16)
    qpst = contextlib.ExitStack()
    p_mm = qpst.enter_context(nc.psum_tensor("p_mm", [128, TC], F32))
    p_sw = qpst.enter_context(nc.psum_tensor("p_sw", [128, TC], F32))
    p_T = qpst.enter_context(nc.psum_tensor("p_T", [128, 64], BF16))

    ev_perm = P.dma("gpsimd", perm_sb[:, :], perm_i.ap()[:, :], deps=ev_endBC)
    ev_id2 = P.dma("gpsimd", id2_sb[:, :], id2_i.ap()[:, :])
    ev_msk = P.dma("gpsimd", masks_sb[:, :], masks_i.ap()[:, :])
    ev_vones = P.op("vector", lambda e: e.memset(v_sb[:, :], 1.0), deps=ev_endBC)

    chunks = [("q", i) for i in range(NOC_Q)] + [("k", i) for i in range(2)] + [("v", i) for i in range(2)]
    ev_w_used = [None, None]
    ev_evac = None
    ev_rope_last = None
    ev_mmdone_prev = None
    ev_vt_done = None
    ev_cs_used = None
    ctx_store_evs = []
    for tc in range(NTC_G):
        tsl = slice(tc * TC, (tc + 1) * TC)
        r_, tcl = tc // 4, tc % 4
        lsl = slice(tcl * TC, (tcl + 1) * TC)
        ev_xnl = P.dma("sync", d_xn[:, :].rearrange("p (c t) -> p c t", t=TC),
                       xn_glob[:, r_, :, lsl],
                       deps=[ev_mmdone_prev, ev_ag_xn1] + (ev_endBC if tc == 0 else []))
        ev_cos = P.dma("sync", cos_sb[:, :], cosr_i.ap()[:, tsl], deps=[ev_cs_used])
        ev_sin = P.dma("sync", sin_sb[:, :], sins_i.ap()[:, tsl], deps=[ev_cs_used])
        for ci, (kind, oc) in enumerate(chunks):
            wbuf = ci % 2
            w_t = {"q": wq_t, "k": wk_t, "v": wv_t}[kind]
            ev_wld = P.dma("scalar", d_w[wbuf][:, :], w_t[:, oc, :],
                           deps=[ev_w_used[wbuf], wcast_evs[kind]])
            for c in range(DC):
                last = c == DC - 1
                P.op("tensor", lambda e, c=c, last=last, wbuf=wbuf: e.matmul(
                    p_mm[:, :], d_w[wbuf][:, c * 128:(c + 1) * 128], d_xn[:, c * TC:(c + 1) * TC],
                    start=(c == 0), stop=last),
                    deps=[ev_wld, ev_xnl, ev_evac] + (ev_endBC if tc == 0 and ci == 0 else []) if c == 0 else (),
                    inc=1 if last else 0)
            ev_mmq = P.last_ev["tensor"]
            ev_w_used[wbuf] = ev_mmq
            if kind == "v":
                ev_evac = P.op("vector", lambda e: e.tensor_copy(v_tmp[:, :], p_mm[:, :]),
                               deps=[ev_mmq, ev_vt_done])
                for half in range(2):
                    g = 2 * oc + half
                    hb = 64 * half
                    for j in range(4):
                        ev_tp = P.op("tensor", lambda e, hb=hb, j=j: e.transpose(
                            p_T[:, :], v_tmp[hb:hb + 64, j * 128:(j + 1) * 128], id2_sb[hb:hb + 64, :]),
                            deps=[ev_evac, ev_id2, ev_vt_done], inc=1)
                        ev_vt_done = P.op("vector", lambda e, g=g, j=j, tc=tc: e.tensor_copy(
                            v_sb[:, (g * 32 + tc * 4 + j) * 65:(g * 32 + tc * 4 + j) * 65 + 64],
                            p_T[:, :]), deps=[ev_tp, ev_vones])
            else:
                ev_evac = P.op("vector", lambda e: e.tensor_copy(qf_bf[:, :], p_mm[:, :]),
                               deps=[ev_mmq, ev_rope_last])
                ev_sw = P.op("tensor", lambda e: e.matmul(
                    p_sw[:, :], perm_sb[:, :], qf_bf[:, :], start=True, stop=True),
                    deps=[ev_evac, ev_perm, ev_rope_last], inc=1)
                ev_At2 = P.op("vector", lambda e: e.tensor_mul(d_At[:, :], qf_bf[:, :], cos_sb[:, :]),
                              deps=[ev_evac, ev_cos])
                if kind == "q":
                    dst_ap = q_sb[:, oc * TF + tc * TC: oc * TF + (tc + 1) * TC]
                    ev_r = P.op("vector", lambda e: e.tensor_mul(krope[:, :], p_sw[:, :], sin_sb[:, :]),
                                deps=[ev_sw, ev_sin])
                    ev_rope_last = P.op("vector", lambda e, dst_ap=dst_ap: e.tensor_add(
                        dst_ap, d_At[:, :], krope[:, :]))
                else:
                    ev_r = P.op("vector", lambda e: e.tensor_mul(krope[:, :], p_sw[:, :], sin_sb[:, :]),
                                deps=[ev_sw, ev_sin])
                    ev_rope_last = P.op("vector", lambda e: e.tensor_add(krope[:, :], d_At[:, :], krope[:, :]))
                    for half in range(2):
                        g = 2 * oc + half
                        hb = 64 * half
                        col = g * TF + tc * TC
                        P.dma("gpsimd", k_sb[0:64, col:col + TC], krope[hb:hb + 64, :], deps=[ev_rope_last])
                        ev_kdup = P.dma("gpsimd", k_sb[64:128, col:col + TC], krope[hb:hb + 64, :])
            ev_cs_used = ev_rope_last
        ev_mmdone_prev = P.last_ev["tensor"]
        next_ghost(1)
    ev_kdup_done = P.last_ev["gpsimd"]
    ev_qkv_done = [P.last_ev["vector"], P.last_ev["tensor"], P.last_ev["scalar"]]
    qpst.close()
    apst = st.enter_context(contextlib.ExitStack())
    p_s = [apst.enter_context(nc.psum_tensor(f"p_s{i}", [128, TC], F32)) for i in range(4)]
    p_ctx = apst.enter_context(nc.psum_tensor("p_ctx", [65, TC], F32))
    p_lb = apst.enter_context(nc.psum_tensor("p_lb", [64, TC], F32))
    p_n = apst.enter_context(nc.psum_tensor("p_n", [1, TC], F32))

    ev_sq_prev = None
    for idx in range(20):
        if idx < 16:
            qh = idx
            base, oc = 64 * (qh % 2), qh // 2
            src = lambda t_, base=base, oc=oc: q_sb[base:base + 64, oc * TF + t_ * TC: oc * TF + (t_ + 1) * TC]
            dep_data = ev_rope_last
        else:
            g = idx - 16
            base = 0
            src = lambda t_, g=g: k_sb[0:64, g * TF + t_ * TC: g * TF + (t_ + 1) * TC]
            dep_data = ev_kdup_done
        for t_ in range(NTC_G):
            ev_q2 = P.op("scalar", lambda e, t_=t_, base=base, src=src: e.activation(
                q2t[base:base + 64, :], src(t_), AF.Square),
                deps=[dep_data, ev_sq_prev] + (ev_qkv_done if idx == 0 and t_ == 0 else []))
            ev_nmm = P.op("tensor", lambda e, base=base: e.matmul(
                p_n[:, :], ones_b[base:base + 64, :], q2t[base:base + 64, :], start=True, stop=True),
                deps=[ev_q2, ev_ob, ev_sq_prev] + (ev_qkv_done if idx == 0 and t_ == 0 else []), inc=1)
            ev_red = P.op("vector", lambda e: e.reduce_max(
                qq[0:1, 31:32], p_n[0:1, :], axis=AX.X), deps=[ev_nmm])
            if t_ == 0:
                ev_mx = P.op("vector", lambda e, idx=idx: e.tensor_copy(qq[0:1, idx:idx + 1], qq[0:1, 31:32]))
            else:
                ev_mx = P.op("vector", lambda e, idx=idx: e.tensor_max(
                    qq[0:1, idx:idx + 1], qq[0:1, idx:idx + 1], qq[0:1, 31:32]))
            ev_sq_prev = ev_mx
    for qh in range(16):
        g = qh // 4
        P.op("vector", lambda e, qh=qh, g=g: e.tensor_mul(
            d_brow[0:1, qh:qh + 1], qq[0:1, qh:qh + 1], qq[0:1, 16 + g:17 + g]), deps=[ev_sq_prev])
    ev_ln2 = P.op("scalar", lambda e: e.activation(d_brow[0:1, :], d_brow[0:1, :], AF.Ln),
                  deps=[P.last_ev["vector"]])
    ev_sqr = P.op("scalar", lambda e: e.activation(d_brow[0:1, :], d_brow[0:1, :], AF.Exp, scale=0.5), deps=[ev_ln2])
    ev_bb2 = P.op("tensor", lambda e: e.matmul(p_lb[:, 0:16], ones_f[0:1, 0:64],
                                               d_brow[0:1, :], start=True, stop=True),
                  deps=[ev_sqr, ev_of], inc=1)
    ev_biasT = P.op("vector", lambda e: e.tensor_scalar(biasT[0:64, :], p_lb[:, 0:16], -0.125, None, ALU.mult),
                    deps=[ev_bb2])
    ev_biasT2 = P.dma("gpsimd", biasT[64:128, :], biasT[0:64, :], deps=[ev_biasT])

    ev_exp_done = [None, None, None, None]
    ev_pv_done = [None, None, None, None]
    ev_ctx_free = None
    ev_lb_free = ev_biasT2
    ev_ctxstore = None
    LKA = 3
    for qh in range(16):
        next_ghost(2)
        g = qh // 4
        base, oc = 64 * (qh % 2), qh // 2
        for qc in range(NTC_G):
            n_kt = 4 * qc + 4
            ev_s_map = {}
            for step in range(n_kt + LKA):
                kt = step
                if kt < n_kt:
                    buf = kt % 4
                    ev_s_map[kt] = P.op("tensor", lambda e, buf=buf, kt=kt, g=g, base=base, oc=oc, qc=qc: e.matmul(
                        p_s[buf][:, :],
                        k_sb[base:base + 64, g * TF + kt * 128: g * TF + (kt + 1) * 128],
                        q_sb[base:base + 64, oc * TF + qc * TC: oc * TF + (qc + 1) * TC],
                        start=True, stop=True),
                        deps=[ev_exp_done[buf], ev_rope_last, ev_kdup_done], inc=1)
                kd = step - LKA
                if kd < 0 or kd >= n_kt:
                    continue
                buf = kd % 4
                ev_e = P.op("scalar", lambda e, buf=buf, qh=qh: e.activation(
                    pT[buf][:, :], p_s[buf][:, :], AF.Exp, scale=0.125, bias=biasT[:, qh:qh + 1]),
                    deps=[ev_s_map.pop(kd), ev_pv_done[buf], ev_biasT2])
                ev_exp_done[buf] = ev_e
                ev_pin = ev_e
                if kd >= 4 * qc:
                    j = kd - 4 * qc
                    ev_pin = P.op("vector", lambda e, buf=buf, j=j: e.tensor_mul(
                        pT[buf][:, :], pT[buf][:, :], masks_sb[:, j * 512:(j + 1) * 512]),
                        deps=[ev_e, ev_msk])
                ev_pv = P.op("tensor", lambda e, buf=buf, kt=kd, g=g, n_kt=n_kt: e.matmul(
                    p_ctx[:, :], v_sb[:, (g * 32 + kt) * 65:(g * 32 + kt) * 65 + 65], pT[buf][:, :],
                    start=(kt == 0), stop=(kt == n_kt - 1)),
                    deps=[ev_pin, ev_vt_done] + ([ev_ctx_free] if kd == 0 else []), inc=1)
                ev_pv_done[buf] = ev_pv
            ev_li = P.op("vector", lambda e: e.reciprocal(lrow[0:1, :], p_ctx[64:65, :]), deps=[ev_pv])
            ev_lb = P.op("tensor", lambda e: e.matmul(
                p_lb[:, :], ones_f[0:1, 0:64], lrow[0:1, :], start=True, stop=True),
                deps=[ev_li, ev_lb_free], inc=1)
            ev_cf = P.op("vector", lambda e: e.tensor_copy(ctxf[:, :], p_ctx[0:64, :]), deps=[ev_pv])
            ev_ctx_free = ev_cf
            ev_ob2 = P.op("vector", lambda e: e.tensor_mul(outbf[:, :], ctxf[:, :], p_lb[:, :]),
                          deps=[ev_lb, ev_cf, ev_ctxstore])
            ev_lb_free = ev_ob2
            ev_ctxstore = P.dma("sync", ctx_d.ap()[qh * 64:(qh + 1) * 64, qc * TC:(qc + 1) * TC],
                                outbf[:, :], deps=[ev_ob2])
            ctx_store_evs.append(ev_ctxstore)

    next_ghost(len(ghost_groups))
    ev_ag_g = P.cc("AllGather", ALU.bypass, ALL8,
                   [cg_d.ap().opt()], [ag_g.ap().opt()], deps=ghost_store_evs[0])
    ev_ag_u = P.cc("AllGather", ALU.bypass, ALL8,
                   [cu_d.ap().opt()], [ag_u.ap().opt()], deps=ghost_store_evs[1])
    ev_ag_d = P.cc("AllGather", ALU.bypass, ALL8,
                   [cd_d.ap().opt()], [ag_d.ap().opt()], deps=ghost_store_evs[2])
    ev_ag_ctx = P.cc("AllGather", ALU.bypass, PAIRS,
                     [ctx_d.ap().opt()], [ag_ctx.ap().opt()], deps=ctx_store_evs)

    ev_endD = P.join()
    apst.close()
    dst.close()

    est = st.enter_context(contextlib.ExitStack())
    esb, eps = sbp(est), psp(est)
    e_xn = esb("e_xn", [128, DC * TC], BF16)
    e_ctx = esb("e_ctx", [128, DC * TC], BF16)
    e_s1 = esb("e_s1", [128, DC * TC], F32)
    e_hdn = esb("e_hdn", [128, IC1 * TC], BF16)
    e_wo = [esb(f"e_wo{i}", [128, DC * 128], BF16) for i in range(2)]
    e_wgu = [esb(f"e_wgu{i}", [128, DC * 128], BF16) for i in range(4)]
    e_wd = [esb(f"e_wd{i}", [128, IC1 * 128], BF16) for i in range(2)]
    e_bo = esb("e_bo", [128, DC], F32)
    e_xs = esb("e_xs", [128, TC], F32)
    e_gg = esb("e_gg", [128, TC], F32)
    e_uu = esb("e_uu", [128, TC], F32)
    e_th = esb("e_th", [128, TC], F32)
    e_t1 = esb("e_t1", [128, TC], F32)
    e_ff = esb("e_ff", [128, TC], F32)
    e_g2 = esb("e_g2", [128, TC], F32)
    e_out = esb("e_out", [128, TC], F32)
    p_eo = eps("p_eo", [128, TC])
    p_eg = eps("p_eg", [128, TC])
    p_eu = eps("p_eu", [128, TC])
    p_ed = eps("p_ed", [128, TC])

    ev_bo = P.dma("gpsimd", e_bo[:, :], bo_i.ap()[:, :], deps=ev_endD)

    ew_used = {}
    ev_s1_done = None
    ev_s1_war = None
    ev_ehdn_used = None
    ev_outstore = None
    ev_eth_r = None
    ev_egg_r = None
    ev_eff_r = None
    ev_evo = ev_evg2 = ev_evu2 = None
    ev_ctxg_prev = None
    for tc in range(NTC_L):
        tsl = slice(tc * TC, (tc + 1) * TC)
        ev_xnl2 = P.dma("sync", e_xn[:, :].rearrange("p (c t) -> p c t", t=TC),
                        xn1_t[:, :, tsl],
                        deps=[P.last_ev["tensor"]] + (ev_endD if tc == 0 else []) + [xn1_store_evs[-1]])
        ctx_g_evs = []
        for c in range(DC):
            evg = P.ig(e_ctx[:, c * TC:(c + 1) * TC], agc_flat, idxp[:, :],
                       c * 128 * TF + tc * TC,
                       deps=[ev_ag_ctx, ev_ctxg_prev, ev_idx] + (ev_endD if tc == 0 and c == 0 else []))
            ctx_g_evs.append(evg)
        ev_ctxg = ctx_g_evs[-1]
        for hc in range(DC):
            wb = hc % 2
            ev_wol = P.dma("scalar", e_wo[wb][:, :], wo_t[:, hc, :],
                           deps=[ew_used.get(("o", wb)), wcast_evs["o"]])
            for c in range(DC):
                last = c == DC - 1
                P.op("tensor", lambda e, c=c, last=last, wb=wb: e.matmul(
                    p_eo[:, :], e_wo[wb][:, c * 128:(c + 1) * 128], e_ctx[:, c * TC:(c + 1) * TC],
                    start=(c == 0), stop=last),
                    deps=[ev_wol, ev_ctxg, ev_xnl2, ev_evo] if c == 0 else (), inc=1 if last else 0)
            ev_mmo = P.last_ev["tensor"]
            ew_used[("o", wb)] = ev_mmo
            ev_evo = P.op("scalar", lambda e, hc=hc: e.activation(
                e_s1[:, hc * TC:(hc + 1) * TC], p_eo[:, :], AF.Identity, bias=e_bo[:, hc:hc + 1]),
                deps=[ev_mmo, ev_bo, ev_evo, ev_s1_war])
        ev_s1_done = ev_evo
        ev_ctxg_prev = P.last_ev["tensor"]
        for ic in range(IC1):
            r_, lc = ic // NB_G, ic % NB_G
            wbg, wbu = ic % 2, 2 + ic % 2
            ev_wgl = P.dma("scalar", e_wgu[wbg][:, :], agg_v[:, r_, lc, :],
                           deps=[ew_used.get(("g", wbg)), ev_ag_g])
            ev_wul = P.dma("scalar", e_wgu[wbu][:, :], agu_v[:, r_, lc, :],
                           deps=[ew_used.get(("u", wbu)), ev_ag_u])
            for c in range(DC):
                last = c == DC - 1
                P.op("tensor", lambda e, c=c, last=last, wbg=wbg: e.matmul(
                    p_eg[:, :], e_wgu[wbg][:, c * 128:(c + 1) * 128], e_xn[:, c * TC:(c + 1) * TC],
                    start=(c == 0), stop=last),
                    deps=[ev_wgl, ev_xnl2, ev_evg2] if c == 0 else (), inc=1 if last else 0)
            ev_mmg = P.last_ev["tensor"]
            ew_used[("g", wbg)] = ev_mmg
            for c in range(DC):
                last = c == DC - 1
                P.op("tensor", lambda e, c=c, last=last, wbu=wbu: e.matmul(
                    p_eu[:, :], e_wgu[wbu][:, c * 128:(c + 1) * 128], e_xn[:, c * TC:(c + 1) * TC],
                    start=(c == 0), stop=last),
                    deps=[ev_wul, ev_xnl2, ev_evu2] if c == 0 else (), inc=1 if last else 0)
            ev_mmu = P.last_ev["tensor"]
            ew_used[("u", wbu)] = ev_mmu
            ev_evg2 = P.op("vector", lambda e: e.tensor_copy(e_gg[:, :], p_eg[:, :]),
                           deps=[ev_mmg, ev_egg_r])
            ev_evu2 = P.op("vector", lambda e: e.tensor_copy(e_uu[:, :], p_eu[:, :]), deps=[ev_mmu])
            ev_eth = P.op("scalar", lambda e: e.activation(e_th[:, :], e_gg[:, :], AF.Tanh, scale=0.5),
                          deps=[ev_evg2, ev_eth_r])
            ev_egg_r = ev_eth
            ev_et1 = P.op("vector", lambda e: e.scalar_tensor_tensor(
                e_t1[:, :], e_th[:, :], 1.0, e_gg[:, :], ALU.add, ALU.mult), deps=[ev_eth])
            ev_eth_r = ev_et1
            P.op("vector", lambda e, ic=ic: e.tensor_mul(
                e_hdn[:, ic * TC:(ic + 1) * TC], e_t1[:, :], e_uu[:, :]), deps=[ev_ehdn_used])
        for hc in range(DC):
            r_, oc2 = hc // 2, hc % 2
            wb = hc % 2
            ev_wdl = P.dma("scalar", e_wd[wb][:, :], agd_v[:, r_, oc2, :],
                           deps=[ew_used.get(("d", wb)), ev_ag_d])
            ev_xsl2 = P.dma("sync", e_xs[:, :], x1_t[:, hc, tsl],
                            deps=[P.last_ev["vector"], x1_store_evs[tc * DC + hc]])
            for ic in range(IC1):
                last = ic == IC1 - 1
                P.op("tensor", lambda e, ic=ic, last=last, wb=wb: e.matmul(
                    p_ed[:, :], e_wd[wb][:, ic * 128:(ic + 1) * 128], e_hdn[:, ic * TC:(ic + 1) * TC],
                    start=(ic == 0), stop=last),
                    deps=[ev_wdl, P.last_ev["vector"]] if ic == 0 else (), inc=1 if last else 0)
            ev_mmd = P.last_ev["tensor"]
            ew_used[("d", wb)] = ev_mmd
            if hc == DC - 1:
                ev_ehdn_used = ev_mmd
            ev_eff = P.op("vector", lambda e: e.tensor_scalar(e_ff[:, :], p_ed[:, :], 0.5, None, ALU.mult),
                          deps=[ev_mmd, ev_eff_r])
            ev_eth2 = P.op("scalar", lambda e: e.activation(e_th[:, :], e_ff[:, :], AF.Tanh, scale=0.5),
                           deps=[ev_eff, ev_eth_r])
            ev_eff_r = ev_eth2
            ev_eg2 = P.op("vector", lambda e: e.scalar_tensor_tensor(
                e_g2[:, :], e_th[:, :], 1.0, e_ff[:, :], ALU.add, ALU.mult), deps=[ev_eth2])
            ev_eth_r = ev_eg2
            ev_esg = P.op("vector", lambda e, hc=hc: e.tensor_mul(
                e_g2[:, :], e_g2[:, :], e_s1[:, hc * TC:(hc + 1) * TC]), deps=[ev_s1_done])
            ev_eo = P.op("vector", lambda e: e.scalar_tensor_tensor(
                e_out[:, :], e_g2[:, :], 0.5, e_xs[:, :], ALU.mult, ALU.add),
                deps=[ev_xsl2, ev_outstore])
            ev_outstore = P.dma("sync", out_t[:, hc, tsl], e_out[:, :], deps=[ev_eo])
            ev_s1_war = ev_esg

    if DEBUG_OUTS:
        allt = {"xn_b": xn_b, "delta_d": delta_d, "bc_d": bc_d, "rows_d": rows_d,
                "ff_d": ff_d, "s_d": s_d, "ag_s": ag_s, "xn1_d": xn1_d,
                "ag_xn1": ag_xn1, "x1_d": x1_d, "ctx_d": ctx_d, "ag_ctx": ag_ctx,
                "cg_d": cg_d, "cu_d": cu_d, "cd_d": cd_d,
                "ag_g": ag_g, "ag_u": ag_u, "ag_d": ag_d,
                "gwt_b": gwt_b, "uwt_b": uwt_b, "dwt_b": dwt_b}
        endall = P.join()
        for name in sorted(DEBUG_OUTS):
            h = allt[name]
            shp = list(h.ap().tensor.shape)
            dbg = nc.dram_tensor("dbg_" + name, shp, h.ap().tensor.dtype,
                                 kind="ExternalOutput")
            P.dma("gpsimd", dbg.ap()[:, :], h.ap()[:, :], deps=endall)


def build_fused():
    nc = bass.Bass("TRN2", target_bir_lowering=False)
    with contextlib.ExitStack() as st:
        P = Prog(nc, st)
        emit_fused(nc, P, st)
        P.final_barrier()
        with nc.Block() as block:
            P.finalize(block)
    return nc


def _rope_consts():
    hd = 64
    inv_freq = 1.0 / (10000.0 ** (np.arange(0, hd, 2, dtype=np.float64) / hd))
    t = np.arange(TF, dtype=np.float64)
    fr = np.outer(t, inv_freq)
    cosT = np.cos(fr).T.astype(np.float32)
    sinT = np.sin(fr).T.astype(np.float32)
    cosr = np.tile(cosT, (4, 1))
    sins = np.concatenate([-sinT, sinT] * 2, axis=0)
    perm = np.zeros((128, 128), np.float32)
    for m_ in range(128):
        b_ = 64 * (m_ // 64)
        src = b_ + (m_ - b_ + 32) % 64
        perm[src, m_] = 1.0
    id2 = np.concatenate([np.eye(64, dtype=np.float32)] * 2, axis=0)
    masks = np.zeros((128, 4 * 512), np.float32)
    for j in range(4):
        for p_ in range(128):
            ts0 = j * 128 + p_
            if ts0 < 512:
                masks[p_, j * 512 + ts0:(j + 1) * 512] = 1.0
    return cosr, sins, perm, id2, masks


def kernel(**inputs):
    nc = build_fused()
    x_full = np.asarray(inputs["L0_attn_x"])
    cosr, sins, perm, id2, masks = _rope_consts()
    onesf = np.ones((1, 128), np.float32)
    def tile4(WT, noc):
        K, M = WT.shape
        assert M == noc * 128
        return np.ascontiguousarray(
            WT.reshape(K // 128, 128, noc, 128).transpose(1, 2, 0, 3).reshape(128, -1))

    xpw_T = np.ascontiguousarray(np.asarray(inputs["L0_attn_x_proj_w"]).T)
    dtw_full = np.asarray(inputs["L0_attn_dt_proj_w"])
    dtb_full = np.asarray(inputs["L0_attn_dt_proj_b"])
    alog_full = np.asarray(inputs["L0_attn_A_log"])
    dp_full = np.asarray(inputs["L0_attn_D_param"])
    gwt = tile4(np.asarray(inputs["L0_ffn_gate_w"]).T, IC0)
    uwt = tile4(np.asarray(inputs["L0_ffn_up_w"]).T, IC0)
    dwt = tile4(np.asarray(inputs["L0_ffn_down_w"]).T, DC)
    gs = np.ascontiguousarray(np.asarray(inputs["L0_ffn_gate_s"]).reshape(IC0, 128).T)
    us = np.ascontiguousarray(np.asarray(inputs["L0_ffn_up_s"]).reshape(IC0, 128).T)
    ds = np.ascontiguousarray(np.asarray(inputs["L0_ffn_down_s"]).reshape(DC, 128).T)
    Wq, Wk, Wv = (np.asarray(inputs["L1_attn_Wq"]), np.asarray(inputs["L1_attn_Wk"]),
                  np.asarray(inputs["L1_attn_Wv"]))
    Wo = np.asarray(inputs["L1_attn_Wo"])
    wo4 = None
    bo = np.ascontiguousarray(np.asarray(inputs["L1_attn_b_out"]).reshape(DC, 128).T)
    giT = np.asarray(inputs["L1_ffn_gate_idx"]).T
    uiT = np.asarray(inputs["L1_ffn_up_idx"]).T
    diT = np.asarray(inputs["L1_ffn_down_idx"]).T
    lut = np.ascontiguousarray(np.asarray(inputs["L1_ffn_lut"]).reshape(1, 16).astype(np.float32))

    def padcols(a, w):
        out = np.zeros((a.shape[0], w), a.dtype)
        take = min(w, a.shape[1])
        out[:, :take] = a[:, :take]
        return out

    wo4 = tile4(Wo, DC)
    maps = []
    for c in range(NCORES):
        b, h = c // 2, c % 2
        xT = np.ascontiguousarray(x_full[b].T)
        m = {
            "x0": np.ascontiguousarray(xT[:, 0:TL]),
            "x1h": np.ascontiguousarray(xT[:, TL:]),
            "xl": np.ascontiguousarray(xT[:, h * TL:(h + 1) * TL]),
            "xg": np.ascontiguousarray(xT[1024 * h:1024 * h + 1024, :]),
            "xpw": xpw_T,
            "dtw": np.ascontiguousarray(dtw_full[1024 * h:1024 * h + 1024, :].T),
            "dtb": np.ascontiguousarray(dtb_full[1024 * h:1024 * h + 1024].reshape(DH, 128).T),
            "alog": np.ascontiguousarray(
                alog_full[1024 * h:1024 * h + 1024].reshape(DH, 128, 16).transpose(1, 0, 2).reshape(128, DH * 16)),
            "dpar": np.ascontiguousarray(dp_full[1024 * h:1024 * h + 1024].reshape(DH, 128).T),
            "gwt": gwt, "uwt": uwt, "dwt": dwt, "gs": gs, "us": us, "ds": ds,
            "wq": tile4(Wq[:, 1024 * h:1024 * h + 1024], 8),
            "wk": tile4(Wk[:, 256 * h:256 * h + 256], 2),
            "wv": tile4(Wv[:, 256 * h:256 * h + 256], 2),
            "wo": wo4, "bo": bo,
            "cosr": cosr, "sins": sins, "perm": perm, "id2": id2, "masks": masks,
            "onesf": onesf,
            "gi": np.ascontiguousarray(padcols(giT[:, GW * c:GW * (c + 1)], GW)),
            "ui": np.ascontiguousarray(padcols(uiT[:, GW * c:GW * (c + 1)], GW)),
            "di": np.ascontiguousarray(diT[:, 256 * c:256 * (c + 1)]),
            "lut": lut,
            "idxp": (2 * np.arange(128, dtype=np.int32) + h).reshape(128, 1),
            "selh": np.array([[1.0 - h, float(h)]], np.float32),
        }
        maps.append(m)
    res = run_bass_kernel_spmd(nc, maps, core_ids=list(range(NCORES)))
    out = np.empty((4, TF, D), np.float32)
    for c in range(NCORES):
        b, h = c // 2, c % 2
        out[b, h * TL:(h + 1) * TL, :] = np.asarray(res.results[c]["out"]).T
    return out


if __name__ == "__main__":
    nc = build_fused()
    n = sum(1 for _ in nc.all_instructions())
    print("built ok, instructions:", n)
